# version 29
# baseline (speedup 1.0000x reference)
import numpy as np
import ml_dtypes
import concourse.bass as bass
import concourse.mybir as mybir
from concourse.bass_utils import run_bass_kernel_spmd

F32 = mybir.dt.float32
BF16 = mybir.dt.bfloat16
AF = mybir.ActivationFunctionType
ALU = mybir.AluOpType
AX = mybir.AxisListType

D = 1024
DT = 8
TSH = 2048
C = 256
NCH = TSH // C
MAGIC = 12582912.0
NEG = -1e30
EPS_LN = 1e-5
BIGEPS = float(np.float32(D * D * EPS_LN))
LN_D = float(np.float32(np.log(np.float32(D))))

ENGS = ("sync", "tensor", "vector", "scalar", "gpsimd")


def build_nc():
    nc = bass.Bass("TRN2", target_bir_lowering=False)

    x_in = nc.dram_tensor("x", [TSH + 1, D], F32, kind="ExternalInput").ap()
    wk_in = nc.dram_tensor("wk", [D, D], F32, kind="ExternalInput").ap()
    wv_in = nc.dram_tensor("wv", [D, D], F32, kind="ExternalInput").ap()
    wr_in = nc.dram_tensor("wr", [D, D], F32, kind="ExternalInput").ap()
    wo_in = nc.dram_tensor("wo", [D, D], F32, kind="ExternalInput").ap()
    wf_in = nc.dram_tensor("wf", [D, D], F32, kind="ExternalInput").ap()
    h_in = nc.dram_tensor("hmat", [D, D], BF16, kind="ExternalInput").ap()
    tmix_in = nc.dram_tensor("tmix", [128, 40], F32, kind="ExternalInput").ap()
    cmask_in = nc.dram_tensor("cmask", [128, 9], F32, kind="ExternalInput").ap()
    eye32_in = nc.dram_tensor("eye32", [128, 128], F32, kind="ExternalInput").ap()
    eyebf_in = nc.dram_tensor("eyebf", [128, 128], BF16, kind="ExternalInput").ap()
    constv_in = nc.dram_tensor("constv", [128, 3], F32, kind="ExternalInput").ap()
    out_d = nc.dram_tensor("out", [TSH, D], F32, kind="ExternalOutput").ap()

    x2d = nc.dram_tensor("x2d", [TSH + 1, D], F32, kind="Internal").ap()
    x3d = nc.dram_tensor("x3d", [TSH, D], F32, kind="Internal").ap()
    wqd = nc.dram_tensor("wqd", [128, DT * D], BF16, kind="Internal").ap()
    cc_in = nc.dram_tensor("cc_in", [128, 16], F32, kind="Internal")
    cc_out = nc.dram_tensor("cc_out", [512, 16], F32, kind="Internal")

    slot = [nc.alloc_sbuf_tensor(f"wslot{i}", [128, DT * D], BF16) for i in range(4)]
    u2s = [nc.alloc_sbuf_tensor(f"u2s{i}", [128, DT * (C + 1)], F32) for i in range(2)]
    pbuf = nc.alloc_sbuf_tensor("pbuf", [128, DT * (C + 1)], F32)
    qbuf = nc.alloc_sbuf_tensor("qbuf", [128, DT * (C + 1)], F32)
    kbuf = nc.alloc_sbuf_tensor("kbuf", [128, DT * C], F32)
    abuf = nc.alloc_sbuf_tensor("abuf", [128, DT * C], F32)
    vbuf = nc.alloc_sbuf_tensor("vbuf", [128, DT * C], F32)
    rtbuf = nc.alloc_sbuf_tensor("rtbuf", [128, DT * C], F32)
    sc1 = nc.alloc_sbuf_tensor("sc1", [128, DT * C], F32)
    sc2 = nc.alloc_sbuf_tensor("sc2", [128, DT * C], F32)
    sc3 = nc.alloc_sbuf_tensor("sc3", [128, DT * C], F32)
    sc4 = nc.alloc_sbuf_tensor("sc4", [128, DT * C], F32)
    mixb = nc.alloc_sbuf_tensor("mixb", [128, DT * C], BF16)
    mixb2 = nc.alloc_sbuf_tensor("mixb2", [128, DT * C], BF16)
    mixb3 = nc.alloc_sbuf_tensor("mixb3", [128, DT * C], BF16)
    ybuf = nc.alloc_sbuf_tensor("ybuf", [128, DT * C], BF16)
    x2c = [nc.alloc_sbuf_tensor(f"x2c{i}", [128, 2 * D], F32) for i in range(3)]
    u2tm = nc.alloc_sbuf_tensor("u2tm", [128, D], F32)
    tmix = nc.alloc_sbuf_tensor("tmix_s", [128, 40], F32)
    tmixg = nc.alloc_sbuf_tensor("tmixg_s", [128, 40], F32)
    cmask = nc.alloc_sbuf_tensor("cmask_s", [128, 9], F32)
    eye32 = nc.alloc_sbuf_tensor("eye32_s", [128, 128], F32)
    eyebf = nc.alloc_sbuf_tensor("eyebf_s", [128, 128], BF16)
    iota = nc.alloc_sbuf_tensor("iota_s", [128, C], F32)
    rstd2 = nc.alloc_sbuf_tensor("rstd2", [128, 17], F32)
    nmr2 = nc.alloc_sbuf_tensor("nmr2", [128, 17], F32)
    smallS = nc.alloc_sbuf_tensor("smallS", [128, 128], F32)
    bnb = nc.alloc_sbuf_tensor("bnb", [128, 12], F32)
    constv = nc.alloc_sbuf_tensor("constv_s", [128, 3], F32)
    sumb = nc.alloc_sbuf_tensor("sumb", [128, 16], F32)
    gath = nc.alloc_sbuf_tensor("gath", [128, 64], F32)

    pE = nc.alloc_psum_tensor("pE", [128, 2048], F32)
    pA = nc.alloc_psum_tensor("pA", [128, 1024], F32)
    pB = nc.alloc_psum_tensor("pB", [128, 1024], F32)
    pBbf = pB[:].bitcast(BF16)
    pAbf = pA[:].bitcast(BF16)
    nc.const_aps.register(F32, BIGEPS, constv[:, 0:1])
    nc.const_aps.register(F32, LN_D, constv[:, 1:2])
    nc.const_aps.register(F32, EPS_LN, constv[:, 2:3])

    sems, cnt, waited = {}, {}, {}
    DX = ["dXa", "dXb", "dXc", "dXd"]
    DO = ["dOa", "dOb", "dOc", "dOd"]
    for s in DX + DO + ["dC", "tS", "vS", "aS", "gS"]:
        sems[s] = nc.alloc_semaphore(s)
        cnt[s] = 0
    pool_idx = {"dX": 0, "dO": 0}
    ops = {e: [] for e in ENGS}

    def wait(eng, sname, val):
        if val <= 0:
            return
        assert val <= cnt[sname], f"wait ahead of inc: {eng} {sname} {val}>{cnt[sname]}"
        key = (eng, sname)
        if waited.get(key, 0) >= val:
            return
        waited[key] = val
        ops[eng].append(("w", lambda e, s=sems[sname], v=val: e.wait_ge(s, v)))

    def run(eng, fn, dep=True):
        ops[eng].append(("o" if dep else "i", fn))

    def inc(eng, sname, fn, dep=True):
        cnt[sname] += 1
        v = cnt[sname]
        ops[eng].append(("o" if dep else "i",
                         lambda e, f=fn, s=sems[sname]: f(e).then_inc(s, 1)))
        return v

    def dma(eng, pool, out_ap, in_ap):
        pname = "dX" if pool is DX else "dO"
        sname = pool[pool_idx[pname] % len(pool)]
        pool_idx[pname] += 1
        wait(eng, sname, cnt[sname])
        cnt[sname] += 16
        v = cnt[sname]
        ops[eng].append(
            ("d", lambda e, o=out_ap, i=in_ap, s=sems[sname]:
             e.dma_start(o, i).then_inc(s, 16)))
        return (sname, v)

    def dwait(eng, tok):
        if tok:
            wait(eng, tok[0], tok[1])

    def dwait_all(eng):
        for s in DX + DO:
            wait(eng, s, cnt[s])

    def barrier(engs=ENGS, sems_list=("tS", "vS", "aS", "gS")):
        for e in engs:
            for s in sems_list:
                wait(e, s, cnt[s])
            dwait_all(e)

    def v3(t, off, n):
        return t[:].rearrange("p (b c) -> p b c", b=DT)[:, :, off:off + n]

    def r8(t):
        return t[:].rearrange("p (b c) -> p b c", b=DT)

    def bc3(col_ap):
        return col_ap.unsqueeze(2).broadcast_to([128, DT, C])

    SP8, SQ8, FA8, FB8, FC8, FD8, PINW8, EW8 = 64, 72, 80, 88, 96, 104, 112, 120
    W8 = 24
    U8 = 32

    def sS(a, b=None):
        return smallS[:, a:a + 1] if b is None else smallS[:, a:b]

    d_cv = dma("sync", DX, constv[:], constv_in)
    d_tm = dma("sync", DX, tmix[:], tmix_in)
    d_tg = dma("sync", DX, tmixg[:], tmix_in)
    d_cm = dma("sync", DX, cmask[:], cmask_in)
    d_e3 = dma("sync", DX, eye32[:], eye32_in)
    d_eb = dma("sync", DX, eyebf[:], eyebf_in)
    d_h1 = dma("sync", DX, slot[0][:].rearrange("p (b j) -> p b j", b=DT),
               h_in.rearrange("(b p) j -> p b j", p=128))
    const_toks = [d_cv, d_tm, d_tg, d_cm, d_e3, d_eb, d_h1]

    def wait_consts(eng):
        for t in const_toks:
            dwait(eng, t)

    g_iota = inc("gpsimd", "gS", lambda g: g.iota(
        iota[:], pattern=[[1, C]], base=0, channel_multiplier=0,
        allow_small_or_imprecise_dtypes=True))
    dwait("scalar", d_tm)
    a_ew = inc("scalar", "aS", lambda s: s.activation(
        out=sS(EW8, EW8 + 8), in_=tmix[:, W8:W8 + 8], func=AF.Exp))

    def emit_wqprep():
        stg = [kbuf[:, 0:1024], kbuf[:, 1024:2048]]
        vq_by_par = [0, 0]
        for et in range(DT):
            par = et % 2
            wait("sync", "vS", vq_by_par[par])
            d_l = dma("sync", DX, stg[par], wf_in[128 * et:128 * et + 128, :])
            dwait("vector", d_l)
            s = stg[par]
            run("vector", lambda v, s=s: v.tensor_reduce(
                out=sS(58), in_=s, axis=AX.X, op=ALU.add, apply_absolute_value=True))
            run("vector", lambda v: v.tensor_scalar(
                out=sS(59), in0=sS(58), scalar1=1.0 / D, scalar2=EPS_LN,
                op0=ALU.mult, op1=ALU.max))
            run("vector", lambda v: v.tensor_scalar(
                out=sS(60), in0=sS(59), scalar1=0.5, scalar2=0.0,
                op0=ALU.mult, op1=ALU.bypass))
            run("vector", lambda v: v.tensor_scalar(
                out=sS(61), in0=sS(59), scalar1=-0.5, scalar2=0.0,
                op0=ALU.mult, op1=ALU.bypass), dep=False)
            run("vector", lambda v, s=s: v.tensor_scalar(
                out=abuf[:, 0:1024], in0=s, scalar1=sS(60), scalar2=0.0,
                op0=ALU.is_gt, op1=ALU.bypass))
            run("vector", lambda v, s=s: v.tensor_scalar(
                out=abuf[:, 1024:2048], in0=s, scalar1=sS(61), scalar2=0.0,
                op0=ALU.is_lt, op1=ALU.bypass), dep=False)
            run("vector", lambda v: v.tensor_tensor(
                out=vbuf[:, 0:1024], in0=abuf[:, 0:1024], in1=abuf[:, 1024:2048],
                op=ALU.subtract))
            vq_by_par[par] = inc("vector", "vS", lambda v, et=et: v.tensor_scalar(
                out=slot[1][:, et * 1024:(et + 1) * 1024], in0=vbuf[:, 0:1024],
                scalar1=sS(59), scalar2=0.0, op0=ALU.mult, op1=ALU.bypass))
        st["vq_all"] = cnt["vS"]
        wait("tensor", "vS", st["vq_all"])
        wait_consts("tensor")
        aq_half = [0, 0]
        aq = 0
        for et in range(DT):
            hb = et % 2
            if aq_half[hb]:
                wait("tensor", "aS", aq_half[hb])
            tq = 0
            for ib in range(DT):
                tq = inc("tensor", "tS", lambda t, et=et, ib=ib, hb=hb: t.transpose(
                    pBbf[:, hb * 1024 + ib * 128: hb * 1024 + (ib + 1) * 128],
                    in_=slot[1][:, et * 1024 + ib * 128: et * 1024 + ib * 128 + 128],
                    identity=eyebf[:]), dep=False)
            wait("scalar", "tS", tq)
            aq = inc("scalar", "aS", lambda s, et=et, hb=hb: s.activation(
                out=slot[2][:].rearrange("p (b c) -> p b c", b=DT)[
                    :, :, et * 128:(et + 1) * 128],
                in_=pBbf[:, hb * 1024:(hb + 1) * 1024].rearrange(
                    "p (b c) -> p b c", b=DT),
                func=AF.Copy))
            aq_half[hb] = aq
        wait("scalar", "aS", aq)
        st["d_wq_stash"] = dma("scalar", DO, wqd, slot[2][:])

    st = {"last_resid": 0, "last_t1evac": 0, "last_t2evac": 0,
          "last_store": {}, "u": 0, "last_t2_t": 0,
          "half_a": [0, 0], "half_v": [0, 0], "fstash": {}, "s2stash": {},
          "stat_free_a": [0, 0, 0, 0]}

    ffn_xts = [x2c[0][:, 0:1024], x2c[1][:, 0:1024],
               x2c[0][:, 1024:2048], x2c[1][:, 1024:2048]]

    def ffn_load(unit, mode, xt_override=None, prev_u=None, extra_dw=(),
                 extra_aS=0):
        u = st["u"]
        st["u"] += 1
        pu = u % 2
        pu4 = u % 4
        xt = xt_override if xt_override is not None else ffn_xts[pu4]
        xfm = mixb2[:, pu * 1024:(pu + 1) * 1024]
        base = pu4 * 16
        cM, cV, cR1, cM32 = base, base + 1, base + 2, base + 3
        if mode == 1:
            src_ap = x_in[0:128, :] if unit == 0 else \
                x_in[1 + 128 * (unit - 1): 129 + 128 * (unit - 1), :]
        else:
            src_ap = x3d[128 * (unit - 1): 128 * (unit - 1) + 128, :]

        prev = st["last_store"].get(u - 4 if prev_u is None else prev_u)
        if prev is not None:
            dwait("sync", prev[0])
            if prev[1]:
                wait("sync", "vS", prev[1])
        for tok in extra_dw:
            dwait("sync", tok)
        if extra_aS:
            wait("sync", "aS", extra_aS)
        d_x = dma("sync", DX, xt, src_ap)
        st["fstash"][u] = dict(unit=unit, mode=mode, pu=pu, pu4=pu4, base=base,
                               hh=u % 2, xt=xt, xfm=xfm, d_x=d_x,
                               cM=cM, cV=cV, cR1=cR1, cM32=cM32)

    def ffn_stats(u):
        fs = st["fstash"][u]
        pu4, xt = fs["pu4"], fs["xt"]
        cM, cV, cR1, cM32 = fs["cM"], fs["cV"], fs["cR1"], fs["cM32"]
        dwait("vector", fs["d_x"])
        wait("vector", "aS", st["stat_free_a"][pu4])
        run("vector", lambda v, xt=xt: v.bn_stats(out=bnb[:, 0:6], in_=xt[:, 0:512]))
        run("vector", lambda v, xt=xt: v.bn_stats(out=bnb[:, 6:12],
                                                  in_=xt[:, 512:1024]), dep=False)
        v_agg = inc("vector", "vS", lambda v: v.bn_aggr(
            out=sS(cM, cM + 2), in_=bnb[:]))
        wait("scalar", "vS", v_agg)
        run("scalar", lambda s: s.activation(
            out=sS(cR1), in_=sS(cV), func=AF.Ln, bias=EPS_LN))
        a_rstd1 = inc("scalar", "aS", lambda s: s.activation(
            out=sS(cR1), in_=sS(cR1), func=AF.Exp, scale=-0.5))
        a_m32 = inc("scalar", "aS", lambda s: s.activation(
            out=sS(cM32), in_=sS(cM), func=AF.Identity, scale=32.0), dep=False)
        fs["a_rstd1"], fs["a_m32"] = a_rstd1, a_m32

    def ffn_t1(u):
        fs = st["fstash"][u]
        xt, xfm = fs["xt"], fs["xfm"]
        dwait("tensor", fs["d_x"])
        wait("tensor", "aS", max(st["last_t1evac"], st["last_t2evac"]))
        t_t1 = 0
        for b in range(DT):
            t_t1 = inc("tensor", "tS", lambda t, xt=xt, b=b: t.transpose(
                pA[:, b * 128:(b + 1) * 128], in_=xt[:, b * 128:(b + 1) * 128],
                identity=eye32[:]), dep=False)
        wait("scalar", "tS", t_t1)
        a_t1 = inc("scalar", "aS", lambda s, xfm=xfm: s.activation(
            out=xfm, in_=pA[:], func=AF.Copy))
        st["last_t1evac"] = a_t1
        fs["a_t1"] = a_t1

    def ffn_hmm(u):
        fs = st["fstash"][u]
        hh, xfm = fs["hh"], fs["xfm"]
        wait("tensor", "aS", max(fs["a_t1"], st["half_a"][hh]))
        wait("tensor", "vS", st["half_v"][hh])
        for jc in range(2):
            for db in range(DT):
                inc("tensor", "tS", lambda t, xfm=xfm, jc=jc, db=db, hh=hh: t.matmul(
                    pE[:, hh * 1024 + jc * 512: hh * 1024 + (jc + 1) * 512],
                    lhsT=xfm[:, db * 128:(db + 1) * 128],
                    rhs=slot[0][:, db * 1024 + jc * 512: db * 1024 + jc * 512 + 512],
                    start=(db == 0), stop=(db == DT - 1)), dep=False)
        fs["t_h"] = cnt["tS"]

    def ffn_quant(u):
        fs = st["fstash"][u]
        pu, hh = fs["pu"], fs["hh"]
        base = fs["base"]
        cM32, cR1 = fs["cM32"], fs["cR1"]
        cMX, cT, cRT, cCC, cCS, cAB = (base + 4, base + 5, base + 6, base + 7,
                                       base + 8, base + 9)
        zb = sc1[:, pu * 1024:(pu + 1) * 1024]
        q1 = sc2[:, pu * 1024:(pu + 1) * 1024]
        Qb = mixb[:, 0:1024]
        hE = pE[:, hh * 1024:(hh + 1) * 1024]

        wait("vector", "tS", max(fs["t_h"], st["last_t2_t"]))
        wait("vector", "aS", fs["a_m32"])
        run("vector", lambda v, hE=hE: v.tensor_reduce(
            out=sS(cMX), in_=hE[:, 1:1024], axis=AX.X, op=ALU.max,
            apply_absolute_value=True), dep=False)
        run("vector", lambda v, hE=hE: v.tensor_scalar(
            out=hE[:, 0:1], in0=hE[:, 0:1], scalar1=sS(cM32), scalar2=0.0,
            op0=ALU.subtract, op1=ALU.bypass), dep=False)
        run("vector", lambda v, hE=hE: v.tensor_scalar(
            out=sS(cAB), in0=hE[:, 0:1], scalar1=-1.0, scalar2=0.0,
            op0=ALU.mult, op1=ALU.max))
        run("vector", lambda v: v.tensor_tensor(
            out=sS(cMX), in0=sS(cMX), in1=sS(cAB), op=ALU.max))
        wait("vector", "aS", fs["a_rstd1"])
        run("vector", lambda v: v.tensor_scalar(
            out=sS(cT), in0=sS(cMX), scalar1=sS(cR1), scalar2=EPS_LN,
            op0=ALU.mult, op1=ALU.max))
        run("vector", lambda v: v.reciprocal(out=sS(cRT), in_=sS(cT)))
        run("vector", lambda v: v.tensor_scalar(
            out=sS(cCC), in0=sS(cRT), scalar1=sS(cR1), scalar2=7.0,
            op0=ALU.mult, op1=ALU.mult))
        v_coef = inc("vector", "vS", lambda v: v.tensor_scalar(
            out=sS(cCS), in0=sS(cT), scalar1=0.5 / 7.0, scalar2=0.0,
            op0=ALU.mult, op1=ALU.bypass), dep=False)
        wait("scalar", "vS", v_coef)
        a_z = inc("scalar", "aS", lambda s, zb=zb, hE=hE: s.activation(
            out=zb, in_=hE, func=AF.Copy, scale=sS(cCC), bias=MAGIC))
        st["half_a"][hh] = a_z
        st["half_v"][hh] = v_coef
        wait("vector", "aS", a_z)
        run("vector", lambda v, zb=zb, q1=q1: v.tensor_scalar(
            out=q1, in0=zb, scalar1=MAGIC, scalar2=-8.0,
            op0=ALU.subtract, op1=ALU.max), dep=False)
        v_q = inc("vector", "vS", lambda v, q1=q1: v.tensor_scalar(
            out=Qb, in0=q1, scalar1=7.0, scalar2=sS(cCS),
            op0=ALU.min, op1=ALU.mult))
        pu4 = fs["pu4"]
        st["stat_free_a"][pu4] = max(st["stat_free_a"][pu4], fs["a_m32"])
        fs["v_q"], fs["a_z"] = v_q, a_z

    def ffn_mm2(u):
        fs = st["fstash"][u]
        unit, mode, pu, hh, xt = fs["unit"], fs["mode"], fs["pu"], fs["hh"], fs["xt"]
        cM, cV = fs["cM"], fs["cV"]
        Qb = mixb[:, 0:1024]
        Qfm = mixb[:, 1024:2048]
        if mode == 1:
            dst = x2d[0:128, :] if unit == 0 else \
                x2d[1 + 128 * (unit - 1): 129 + 128 * (unit - 1), :]
            stat_col = unit
            wslot_q = 2
        else:
            dst = out_d[128 * (unit - 1): 128 * (unit - 1) + 128, :]
            stat_col = None
            wslot_q = 1
        wait("tensor", "vS", fs["v_q"])
        wait("tensor", "aS", st["last_t1evac"])
        t_t2 = 0
        for b in range(DT):
            t_t2 = inc("tensor", "tS", lambda t, b=b: t.transpose(
                pAbf[:, b * 128:(b + 1) * 128],
                in_=Qb[:, b * 128:(b + 1) * 128], identity=eyebf[:]), dep=False)
        st["last_t2_t"] = t_t2
        wait("scalar", "tS", t_t2)
        a_t2 = inc("scalar", "aS", lambda s: s.activation(
            out=Qfm, in_=pAbf[:, 0:1024], func=AF.Copy))
        st["last_t2evac"] = a_t2
        wait("tensor", "aS", a_t2)
        wait("tensor", "vS", st["last_resid"])
        t_w = 0
        for oc in range(2):
            for ib in range(DT):
                t_w = inc("tensor", "tS", lambda t, oc=oc, ib=ib, ws=wslot_q: t.matmul(
                    pB[:, oc * 512:(oc + 1) * 512],
                    lhsT=Qfm[:, ib * 128:(ib + 1) * 128],
                    rhs=slot[ws][:, ib * 1024 + oc * 512: ib * 1024 + oc * 512 + 512],
                    start=(ib == 0), stop=(ib == DT - 1)), dep=False)
        fs["t_w"] = t_w
        fs["dst"] = dst

    def ffn_resid(u):
        fs = st["fstash"].pop(u)
        unit, mode, xt, dst = fs["unit"], fs["mode"], fs["xt"], fs["dst"]
        wait("vector", "tS", fs["t_w"])
        v_res = inc("vector", "vS", lambda v, xt=xt: v.tensor_tensor(
            out=xt, in0=xt, in1=pB[:], op=ALU.add), dep=False)
        st["last_resid"] = v_res
        wait("scalar", "vS", v_res)
        if mode == 1 and unit == 0:
            d_s = dma("scalar", DO, x2d[0:1, :], xt[0:1, :])
        else:
            d_s = dma("scalar", DO, dst, xt)
        st["last_store"][u] = (d_s, 0)
        fs["d_s"] = d_s
        st["s2stash"][u] = fs

    def ffn_stat2(u):
        fs = st["s2stash"].pop(u)
        if fs["mode"] != 1:
            return
        pu4, xt, unit = fs["pu4"], fs["xt"], fs["unit"]
        cM, cV = fs["cM"], fs["cV"]
        stat_col = unit
        run("vector", lambda v, xt=xt: v.bn_stats(out=bnb[:, 0:6],
                                                  in_=xt[:, 0:512]))
        run("vector", lambda v, xt=xt: v.bn_stats(out=bnb[:, 6:12],
                                                  in_=xt[:, 512:1024]), dep=False)
        v_agg2 = inc("vector", "vS", lambda v: v.bn_aggr(
            out=sS(cM, cM + 2), in_=bnb[:]))
        wait("scalar", "vS", v_agg2)
        run("scalar", lambda s, sc=stat_col: s.activation(
            out=rstd2[:, sc:sc + 1], in_=sS(cV), func=AF.Ln, bias=EPS_LN))
        a_r2 = inc("scalar", "aS", lambda s, sc=stat_col: s.activation(
            out=rstd2[:, sc:sc + 1], in_=rstd2[:, sc:sc + 1],
            func=AF.Exp, scale=-0.5))
        wait("vector", "aS", a_r2)
        v_stat2 = inc("vector", "vS", lambda v, sc=stat_col: v.tensor_scalar(
            out=nmr2[:, sc:sc + 1], in0=sS(cM),
            scalar1=rstd2[:, sc:sc + 1], scalar2=-1.0,
            op0=ALU.mult, op1=ALU.mult), dep=False)
        st["stat_free_a"][pu4] = max(st["stat_free_a"][pu4], a_r2)
        st["last_store"][u] = (fs["d_s"], v_stat2)

    at = {"u2_prev_evac": 0, "x3_store": {}, "last_revac_et": [0] * DT,
          "last_resid": 0, "u2_free_g": {0: 0, 1: 0}, "u2_free_v": {0: 0, 1: 0},
          "last_u2t": 0, "x2c_a": {0: 0, 1: 0, 2: 0}, "mslot": 0,
          "tail_v": {}, "u2_ev": {}, "wo_stash": {}, "wores": {},
          "w_ready": {},
          "abuf_free_v": 0}


    def emit_proj(ws, wait_g, prev_evacs, mb):
        tks = []
        wait("tensor", "aS", at["w_ready"].get(ws, 0))
        wait("tensor", "gS", wait_g)
        for et in range(DT):
            wait("tensor", "aS", prev_evacs[et])
            tm = 0
            for db in range(DT):
                tm = inc("tensor", "tS", lambda t, et=et, db=db, ws=ws, mb=mb:
                         t.matmul(
                    pE[:, et * C:(et + 1) * C],
                    lhsT=slot[ws][:, db * 1024 + et * 128: db * 1024 + et * 128 + 128],
                    rhs=mb[:, db * C:(db + 1) * C],
                    start=(db == 0), stop=(db == DT - 1)), dep=False)
            tks.append(tm)
        return tks

    def emit_evacs(tks, out_t, func, scale=1.0):
        evs = []
        for ep in range(DT // 2):
            wait("scalar", "tS", tks[2 * ep + 1])
            e = inc("scalar", "aS", lambda s, ep=ep, o=out_t, f=func,
                    sc=scale: s.activation(
                out=o[:, ep * 2 * C:(ep + 1) * 2 * C],
                in_=pE[:, ep * 2 * C:(ep + 1) * 2 * C],
                func=f, scale=sc), dep=False)
            evs.extend([e, e])
        return evs

    def emit_mix8(base_col, mb, u2c, wait_t):
        if wait_t:
            wait("gpsimd", "tS", wait_t)
        run("gpsimd", lambda g, bc=base_col, mb=mb: g.tensor_tensor(
            out=r8(mb), in0=r8(sc1), in1=bc3(tmixg[:, bc:bc + 8]), op=ALU.mult))
        gm = inc("gpsimd", "gS", lambda g, mb=mb, u2c=u2c: g.tensor_tensor(
            out=r8(mb), in0=r8(mb), in1=v3(u2c, 0, C), op=ALU.add))
        return gm

    def attn_u2(j, redo=False):
        pu = 0 if redo else j % 2
        bu = 0 if redo else j % 3
        u2c = u2s[pu]
        u2p = u2s[1 - pu]
        first = (j == 0)

        def dw_store(u, full=False):
            p = st["last_store"].get(u)
            if p is not None:
                dwait("sync", p[0])
                if full and p[1]:
                    wait("sync", "vS", p[1])

        if redo:
            dwait("sync", at["x3_store"][NCH - 2])
            dwait("sync", at["x3_store"][NCH - 1])
        else:
            xs = at["x3_store"].get(j - 3)
            if xs is not None:
                dwait("sync", xs)
            dw_store(2 * j + 1)
            dw_store(2 * j + 2)
            if j == 0:
                dw_store(14, full=True)
                dw_store(16, full=True)
                dw_store(13, full=True)
                dw_store(0)
            if j == 1:
                dw_store(13, full=True)
                dw_store(15, full=True)
        wait("sync", "aS", at["x2c_a"][bu])
        if first:
            wait("sync", "aS", at["x2c_a"][1])
        d_x2 = dma(
            "sync", DX,
            x2c[bu][:].rearrange("p (a d) -> p a d", a=2),
            x2d[1 + C * j: 1 + C * (j + 1), :].rearrange("(a p) d -> p a d", p=128))
        d_xh = None
        if first:
            d_xh = dma("sync", DX, x2c[1][:, 0:1024], x2d[0:128, :])
            at["d_xh"] = d_xh

        def u2_build(src_ap, stat_col, off, d_dep, width=128):
            dwait("scalar", d_dep)
            wait("scalar", "gS", at["u2_free_g"][pu])
            wait("scalar", "vS", at["u2_free_v"][pu])
            wait("scalar", "tS", at["last_u2t"])
            a_ap = inc("scalar", "aS", lambda s, src=src_ap, sc=stat_col: s.activation(
                out=u2tm[:], in_=src, func=AF.Identity,
                scale=rstd2[:, sc:sc + 1], bias=nmr2[:, sc:sc + 1]))
            wait("tensor", "aS", max(a_ap, at["u2_prev_evac"]))
            tt = 0
            for b in range(DT):
                tt = inc("tensor", "tS", lambda t, b=b: t.transpose(
                    pA[:, b * 128:(b + 1) * 128], in_=u2tm[:, b * 128:(b + 1) * 128],
                    identity=eye32[:]), dep=False)
            at["last_u2t"] = tt
            wait("scalar", "tS", tt)
            ev = inc("scalar", "aS", lambda s, off=off, w=width, u2c=u2c: s.activation(
                out=v3(u2c, off, w),
                in_=pA[:].rearrange("p (b c) -> p b c", b=DT)[:, :, 0:w],
                func=AF.Copy))
            at["u2_prev_evac"] = ev
            return ev

        if first:
            ev = u2_build(x2c[1][:, 0:1024], 0, 0, d_xh, width=1)
            at["x2c_a"][1] = ev
            ev = u2_build(x2c[bu][:, 0:1024], 1, 1, d_x2)
            ev = u2_build(x2c[bu][:, 1024:2048], 2, 129, d_x2)
            at["x2c_a"][bu] = ev
        else:
            wait("gpsimd", "aS", at["u2_prev_evac"])
            wait("gpsimd", "vS", max(at["u2_free_v"][1 - pu],
                                     at["u2_free_v"][pu]))
            wait("gpsimd", "gS", at["u2_free_g"][pu])
            g_h = inc("gpsimd", "gS", lambda g, u2c=u2c, u2p=u2p: g.tensor_copy(
                v3(u2c, 0, 1), v3(u2p, C, 1)))
            at["u2_free_g"][1 - pu] = g_h
            at["halo_g"] = g_h
            ev = u2_build(x2c[bu][:, 0:1024], 2 * j + 1, 1, d_x2)
            ev = u2_build(x2c[bu][:, 1024:2048], 2 * j + 2, 129, d_x2)
            at["x2c_a"][bu] = ev
        u2cv = u2c[:].rearrange("p (b c) -> p b c", b=DT)
        wait_consts("gpsimd")
        wait("gpsimd", "aS", ev)
        wait("gpsimd", "vS", at.get("sc1_dve_free", 0))
        inc("gpsimd", "gS", lambda g, u2cv=u2cv: g.tensor_tensor(
            out=r8(sc1), in0=u2cv[:, :, 1:1 + C], in1=u2cv[:, :, 0:C],
            op=ALU.subtract))
        v_xk = emit_mix8(0, mixb, u2c,
                         max(at.get("t_k_prev", 0), at.get("mix_pe_free", 0)))
        v_xv = emit_mix8(8, mixb2, u2c,
                         max(at.get("t_v_prev", 0), at.get("mix_pe_free", 0)))
        at["u2_ev"][j] = (ev, pu, v_xk, v_xv)

    def attn_kxx_k(j, redo=False):
        ev, pu, v_xk, v_xv = at["u2_ev"].pop(j)
        u2c = u2s[pu]
        at["u2c"] = u2c
        at["pu"] = pu
        at["v_xv"] = v_xv
        wait("tensor", "vS", at.get("pE_wo_free", 0))
        t_k = emit_proj(0, v_xk, at["last_revac_et"], mixb)
        at["t_k_prev"] = t_k[DT - 1]
        wait("scalar", "gS", at.get("g_a", 0))
        a_k = emit_evacs(t_k, kbuf, AF.Copy)
        at["t_k"], at["a_k"] = t_k, a_k
        at["first"] = (j == 0)
        at["redo"] = redo
        at["j"] = j

    def attn_kxx_v(j):
        a_k = at["a_k"]
        v_xv = at["v_xv"]
        wait("gpsimd", "vS", max(at.get("tail_vg", 0), at.get("v_qs", 0)))
        if j == 0:
            g_pb = inc("gpsimd", "gS", lambda g: g.memset(v3(pbuf, 0, 1), NEG))
            wait("gpsimd", "vS", at.get("tail_vn1", 0))
            g_qb = inc("gpsimd", "gS", lambda g: g.memset(v3(qbuf, 0, 1), 0.0))
        else:
            g_pb = inc("gpsimd", "gS", lambda g: g.tensor_copy(
                v3(pbuf, 0, 1), v3(pbuf, C, 1)))
            wait("gpsimd", "vS", at.get("tail_vn1", 0))
            g_qb = inc("gpsimd", "gS", lambda g: g.tensor_copy(
                v3(qbuf, 0, 1), v3(qbuf, C, 1)))
        at["g_pb"], at["g_qb"] = g_pb, g_qb
        if not (j == 0 and not at["redo"]):
            wait("gpsimd", "tS", at.get("wprep_t", 0))
            wait("gpsimd", "aS", a_k[DT - 1])
            wait("gpsimd", "vS", at["abuf_free_v"])
            g_a = inc("gpsimd", "gS", lambda g: g.tensor_tensor(
                out=r8(abuf), in0=r8(kbuf), in1=bc3(tmixg[:, U8:U8 + 8]),
                op=ALU.add))
            at["g_a"] = g_a
        t_v = emit_proj(2, v_xv, a_k, mixb2)
        at["t_v_prev"] = t_v[DT - 1]
        wait("scalar", "vS", at.get("vbuf_free", 0))
        a_v = emit_evacs(t_v, vbuf, AF.Copy)
        at["t_v"], at["a_v"] = t_v, a_v

    def attn_mixr(j):
        at["v_xr"] = emit_mix8(16, mixb3, at["u2c"], at.get("t_r_prev", 0))

    def attn_head_late(j):
        first, redo = at["first"], at["redo"]
        t_k, a_k = at["t_k"], at["a_k"]
        pu = at["pu"]
        u2c = at["u2c"]

        wait("vector", "gS", at["g_pb"])
        for b in range(DT):
            wait("vector", "aS", a_k[b])
            inc("vector", "vS", lambda v, b=b: v.tensor_tensor_scan(
                out=pbuf[:, b * (C + 1) + 1: b * (C + 1) + 1 + C],
                data0=tmix[:, W8 + b:W8 + b + 1].broadcast_to([128, C]),
                data1=kbuf[:, b * C:(b + 1) * C],
                initial=pbuf[:, b * (C + 1): b * (C + 1) + 1],
                op0=ALU.add, op1=ALU.max), dep=(b == 0))
        t_v, a_v = at["t_v"], at["a_v"]
        run("vector", lambda v: v.tensor_tensor(
            out=r8(sc2), in0=v3(pbuf, 0, C), in1=bc3(tmix[:, W8:W8 + 8]),
            op=ALU.add))
        v_al0 = inc("vector", "vS", lambda v: v.tensor_tensor(
            out=r8(sc3), in0=r8(sc2), in1=v3(pbuf, 1, C), op=ALU.subtract))
        wait("scalar", "vS", v_al0)
        a_ea = inc("scalar", "aS", lambda s: s.activation(
            out=sc3[:], in_=sc3[:], func=AF.Exp))
        v_be = inc("vector", "vS", lambda v: v.tensor_tensor(
            out=r8(sc2), in0=r8(kbuf), in1=v3(pbuf, 1, C), op=ALU.subtract))
        wait("scalar", "vS", v_be)
        a_be = inc("scalar", "aS", lambda s: s.activation(
            out=sc2[:], in_=sc2[:], func=AF.Exp))
        wait("vector", "aS", max(a_be, a_v[DT - 1]))
        v_bv = inc("vector", "vS", lambda v: v.tensor_tensor(
            out=sc2[:], in0=sc2[:], in1=vbuf[:], op=ALU.mult), dep=False)
        at["vbuf_free"] = max(at.get("vbuf_free", 0), v_bv)
        wait("vector", "aS", a_ea)
        wait("vector", "gS", at["g_qb"])
        v_qs = 0
        for b in range(DT):
            v_qs = inc("vector", "vS", lambda v, b=b: v.tensor_tensor_scan(
                out=qbuf[:, b * (C + 1) + 1: b * (C + 1) + 1 + C],
                data0=sc3[:, b * C:(b + 1) * C],
                data1=sc2[:, b * C:(b + 1) * C],
                initial=qbuf[:, b * (C + 1): b * (C + 1) + 1],
                op0=ALU.mult, op1=ALU.add), dep=(b == 0))
        at["v_qs"] = v_qs

    def attn_head_b(j):
        redo = at["redo"]
        pu = at["pu"]
        a_v = at["a_v"]
        t_r = emit_proj(1, at["v_xr"], a_v, mixb3)
        at["t_r_prev"] = t_r[DT - 1]
        wait("scalar", "vS", at.get("last_tail_v", 0))
        a_r = emit_evacs(t_r, rtbuf, AF.Copy)
        at["last_revac_et"] = a_r
        at["a_r"] = a_r
        at["u2_free_g"][pu] = max(at["u2_free_g"][pu], at["v_xr"])
        wait("vector", "aS", a_r[DT - 1])
        v_hr = inc("vector", "vS", lambda v: v.tensor_scalar(
            out=rtbuf[:], in0=rtbuf[:], scalar1=-1.0, scalar2=30.0,
            op0=ALU.mult, op1=ALU.min))
        wait("scalar", "vS", v_hr)
        at["a_er"] = inc("scalar", "aS", lambda s: s.activation(
            out=rtbuf[:], in_=rtbuf[:], func=AF.Exp))

        if redo:
            wait("vector", "gS", max(g_iota, at["v_xr"]))
            run("vector", lambda v: v.tensor_tensor(
                out=r8(sc1), in0=iota[:].unsqueeze(1).broadcast_to([128, DT, C]),
                in1=bc3(tmix[:, W8:W8 + 8]), op=ALU.mult))
            run("vector", lambda v: v.tensor_tensor(
                out=r8(sc1), in0=r8(sc1), in1=bc3(sS(PINW8, PINW8 + 8)),
                op=ALU.add))
            run("vector", lambda v: v.tensor_tensor(
                out=r8(sc2), in0=r8(sc1), in1=v3(pbuf, 1, C), op=ALU.subtract))
            v_pc = inc("vector", "vS", lambda v: v.tensor_tensor(
                out=v3(pbuf, 1, C), in0=v3(pbuf, 1, C), in1=r8(sc1), op=ALU.max))
            wait("scalar", "vS", v_pc)
            run("scalar", lambda s: s.activation(out=sc3[:], in_=sc2[:], func=AF.Relu))
            run("scalar", lambda s: s.activation(
                out=sc3[:], in_=sc3[:], func=AF.Exp, scale=-1.0))
            run("scalar", lambda s: s.activation(
                out=sc2[:], in_=sc2[:], func=AF.Relu, scale=-1.0))
            a_ed = inc("scalar", "aS", lambda s: s.activation(
                out=sc2[:], in_=sc2[:], func=AF.Exp, scale=-1.0))
            wait("vector", "aS", a_ed)
            run("vector", lambda v: v.tensor_tensor(
                out=r8(sc3), in0=r8(sc3), in1=v3(qbuf, 1, C), op=ALU.mult),
                dep=False)
            run("vector", lambda v: v.tensor_tensor(
                out=r8(sc2), in0=r8(sc2), in1=bc3(sS(SQ8, SQ8 + 8)), op=ALU.mult),
                dep=False)
            run("vector", lambda v: v.tensor_tensor(
                out=v3(qbuf, 1, C), in0=r8(sc2), in1=r8(sc3), op=ALU.add))
            run("vector", lambda v: v.tensor_copy(v3(pbuf, 0, 1), sS(SP8, SP8 + 8)))
            run("vector", lambda v: v.tensor_copy(
                v3(qbuf, 0, 1), sS(SQ8, SQ8 + 8)), dep=False)

    def attn_tail(j):
        bu = 0 if at["redo"] and at["j"] == j else j % 3
        g_a = at["g_a"]
        a_r = at["a_r"]
        wait("vector", "gS", g_a)
        v_g = inc("vector", "vS", lambda v: v.tensor_tensor(
            out=r8(sc4), in0=r8(abuf), in1=v3(pbuf, 0, C), op=ALU.subtract))
        at["tail_vg"] = v_g
        run("vector", lambda v: v.tensor_scalar(
            out=sc3[:], in0=sc4[:], scalar1=0.0, scalar2=0.0,
            op0=ALU.max, op1=ALU.bypass))
        v_gn = inc("vector", "vS", lambda v: v.tensor_scalar(
            out=sc4[:], in0=sc4[:], scalar1=0.0, scalar2=0.0,
            op0=ALU.min, op1=ALU.bypass))
        wait("scalar", "vS", v_gn)
        run("scalar", lambda s: s.activation(
            out=sc3[:], in_=sc3[:], func=AF.Exp, scale=-1.0))
        a_esc = inc("scalar", "aS", lambda s: s.activation(
            out=sc4[:], in_=sc4[:], func=AF.Exp))
        wait("vector", "aS", a_esc)
        v_den = inc("vector", "vS", lambda v: v.scalar_tensor_tensor(
            out=abuf[:], in0=sc3[:], scalar=1e-9, in1=sc4[:],
            op0=ALU.add, op1=ALU.add), dep=False)
        wait("vector", "aS", at["a_er"])
        v_d2 = inc("vector", "vS", lambda v: v.scalar_tensor_tensor(
            out=sc2[:], in0=rtbuf[:], scalar=1.0, in1=abuf[:],
            op0=ALU.add, op1=ALU.mult))
        at["abuf_free_v"] = v_d2
        wait("scalar", "vS", v_d2)
        run("scalar", lambda s: s.activation(out=sc2[:], in_=sc2[:], func=AF.Ln))
        a_rd = inc("scalar", "aS", lambda s: s.activation(
            out=sc2[:], in_=sc2[:], func=AF.Exp, scale=-1.0))
        v_n1 = inc("vector", "vS", lambda v: v.tensor_tensor(
            out=r8(sc3), in0=r8(sc3), in1=v3(qbuf, 0, C), op=ALU.mult),
            dep=False)
        at["tail_vn1"] = v_n1
        run("vector", lambda v: v.tensor_tensor(
            out=sc4[:], in0=sc4[:], in1=vbuf[:], op=ALU.mult), dep=False)
        v_num = inc("vector", "vS", lambda v: v.tensor_tensor(
            out=sc3[:], in0=sc3[:], in1=sc4[:], op=ALU.add))
        at["vbuf_free"] = max(at.get("vbuf_free", 0), v_num)
        wait("vector", "aS", a_rd)
        wait("vector", "tS", at.get("wo_t1", 0))
        v_y = inc("vector", "vS", lambda v: v.tensor_tensor(
            out=ybuf[:], in0=sc3[:], in1=sc2[:], op=ALU.mult))
        at["tail_v"][j] = v_y
        at["last_tail_v"] = v_y
        at["wo_stash"][j] = (v_y, bu)

    def attn_wo_mm(j):
        v_y, bu = at["wo_stash"].pop(j)
        wait("tensor", "aS", at["w_ready"].get(3, 0))
        wait("tensor", "vS", v_y)
        wait("tensor", "vS", at["last_resid"])
        wait("tensor", "aS", at["a_r"][3])
        t0 = 0
        for oc in range(2):
            for db in range(DT):
                t0 = inc("tensor", "tS", lambda t, oc=oc, db=db: t.matmul(
                    pE[:, oc * 512:(oc + 1) * 512],
                    lhsT=ybuf[:, db * C: db * C + 128],
                    rhs=slot[3][:, db * 1024 + oc * 512: db * 1024 + oc * 512 + 512],
                    start=(db == 0), stop=(db == DT - 1)), dep=False)
        t1 = 0
        for oc in range(2):
            for db in range(DT):
                t1 = inc("tensor", "tS", lambda t, oc=oc, db=db: t.matmul(
                    pB[:, oc * 512:(oc + 1) * 512],
                    lhsT=ybuf[:, db * C + 128: db * C + 256],
                    rhs=slot[3][:, db * 1024 + oc * 512: db * 1024 + oc * 512 + 512],
                    start=(db == 0), stop=(db == DT - 1)), dep=False)
        at["wores"][j] = (bu, t0, t1)
        at["wo_t1"] = t1

    def attn_wo_res(j):
        bu, t0, t1 = at["wores"].pop(j)
        wait("vector", "tS", t1)
        v_r2a = inc("vector", "vS", lambda v, bu=bu: v.tensor_tensor(
            out=x2c[bu][:, 0:1024], in0=x2c[bu][:, 0:1024],
            in1=pE[:, 0:1024], op=ALU.add))
        v_r2b = inc("vector", "vS", lambda v, bu=bu: v.tensor_tensor(
            out=x2c[bu][:, 1024:2048], in0=x2c[bu][:, 1024:2048],
            in1=pB[:], op=ALU.add), dep=False)
        at["pE_wo_free"] = v_r2a
        at["last_resid"] = v_r2b
        wait("scalar", "vS", v_r2b)
        d_x3 = dma(
            "scalar", DO,
            x3d[C * j: C * (j + 1), :].rearrange("(a p) d -> p a d", p=128),
            x2c[bu][:].rearrange("p (a d) -> p a d", a=2))
        at["x3_store"][j] = d_x3

    dwait("scalar", d_cv)
    wait_consts("vector")
    wait_consts("scalar")
    wait_consts("tensor")
    U0 = st["u"]
    ffn_load(0, mode=1)
    ffn_t1(U0)
    ffn_hmm(U0)
    ffn_load(1, mode=1)
    ffn_stats(U0)
    ffn_t1(U0 + 1)
    ffn_hmm(U0 + 1)
    ffn_load(2, mode=1)
    ffn_stats(U0 + 1)
    ffn_load(3, mode=1)
    emit_wqprep()
    ffn_quant(U0)
    ffn_mm2(U0)
    for k in range(2, 17):
        fu = U0 + k
        ffn_t1(fu)
        ffn_resid(fu - 2)
        ffn_quant(fu - 1)
        ffn_hmm(fu)
        ffn_mm2(fu - 1)
        ffn_stat2(fu - 2)
        if k + 2 < 17:
            ffn_load(k + 2, mode=1)
        ffn_stats(fu)
    ffn_resid(U0 + 15)
    ffn_quant(U0 + 16)
    ffn_mm2(U0 + 16)
    ffn_stat2(U0 + 15)
    ffn_resid(U0 + 16)
    ffn_stat2(U0 + 16)

    wp_t_par = [0, 0]
    wp_last_a = [0]

    def emit_wprep(src, dst_slot, scale, evac_wait_t=0, evac_wait_dO=0):
        stg2 = [abuf[:, 0:1024], abuf[:, 1024:2048]]
        t_by_par = wp_t_par
        wait("sync", "vS", st["vq_all"])
        wait("tensor", "vS", st["last_resid"])
        for et in range(DT):
            par = et % 2
            wait("sync", "tS", t_by_par[par])
            d_l = dma("sync", DX, stg2[par], src[128 * et:128 * et + 128, :])
            dwait("tensor", d_l)
            for half in range(2):
                if wp_last_a[0]:
                    wait("tensor", "aS", wp_last_a[0])
                tq2 = 0
                for b in range(4):
                    ib = half * 4 + b
                    tq2 = inc("tensor", "tS", lambda t, ib=ib, b=b, par=par:
                              t.transpose(pB[:, b * 128:(b + 1) * 128],
                                          in_=stg2[par][:, ib * 128:(ib + 1) * 128],
                                          identity=eye32[:]), dep=False)
                t_by_par[par] = tq2
                wait("scalar", "tS", tq2)
                if evac_wait_t:
                    wait("scalar", "tS", evac_wait_t)
                if evac_wait_dO:
                    dwait("scalar", evac_wait_dO)
                wp_last_a[0] = inc("scalar", "aS", lambda s, et=et, half=half,
                                   d=dst_slot, sc=scale: s.activation(
                    out=slot[d][:].rearrange("p (b c) -> p b c", b=DT)[
                        :, half * 4:half * 4 + 4, et * 128:(et + 1) * 128],
                    in_=pB[:, 0:512].rearrange("p (b c) -> p b c", b=4),
                    func=AF.Copy, scale=sc))
        return wp_last_a[0]

    ffn1_last_t = cnt["tS"]
    at["sc1_dve_free"] = cnt["vS"]
    at["mix_pe_free"] = cnt["tS"]
    attn_u2(0)
    at["w_ready"][0] = emit_wprep(wk_in, 0, 1.0, evac_wait_t=ffn1_last_t)
    attn_kxx_k(0)
    at["w_ready"][2] = emit_wprep(wv_in, 2, 1.0, evac_wait_dO=st["d_wq_stash"])
    attn_kxx_v(0)
    at["w_ready"][1] = emit_wprep(wr_in, 1, 1.0)
    attn_mixr(0)
    attn_head_late(0)
    attn_head_b(0)
    attn_u2(1)
    at["w_ready"][3] = emit_wprep(wo_in, 3, 1.0)
    at["wprep_t"] = cnt["tS"]
    for j in range(1, NCH):
        attn_kxx_k(j)
        attn_kxx_v(j)
        attn_mixr(j)
        attn_head_late(j)
        attn_head_b(j)
        if j + 1 < NCH:
            attn_u2(j + 1)
        if j >= 2:
            attn_wo_mm(j - 1)
            attn_wo_res(j - 1)
        attn_tail(j)
    attn_wo_mm(NCH - 1)
    attn_wo_res(NCH - 1)

    run("vector", lambda v: v.tensor_copy(sumb[:, 0:8], v3(pbuf, C, 1)))
    v_sum = inc("vector", "vS", lambda v: v.tensor_copy(
        sumb[:, 8:16], v3(qbuf, C, 1)), dep=False)
    wait("scalar", "vS", v_sum)
    d_sum = dma("scalar", DO, cc_in.ap(), sumb[:])
    dwait("gpsimd", d_sum)
    cnt["dC"] += 1
    ops["gpsimd"].append(("o", lambda g: g.collective_compute(
        "AllGather", ALU.bypass,
        replica_groups=[[0, 1, 2, 3], [4, 5, 6, 7]],
        ins=[cc_in.ap()], outs=[cc_out.ap()]).then_inc(sems["dC"])))
    wait("sync", "dC", cnt["dC"])
    d_gs = []
    for r in range(4):
        d_gs.append(dma("sync", DX, gath[:, 16 * r:16 * r + 16],
                        cc_out.ap()[128 * r:128 * r + 128, :]))
    for t in d_gs:
        dwait("gpsimd", t)
    run("gpsimd", lambda g: g.memset(sS(SP8, SP8 + 8), NEG))
    g_f = inc("gpsimd", "gS", lambda g: g.memset(sS(SQ8, SQ8 + 8), 0.0))
    for t in d_gs:
        dwait("vector", t)
    wait("vector", "gS", g_f)
    for jj in range(3):
        run("vector", lambda v, jj=jj: v.tensor_scalar(
            out=sS(FA8, FA8 + 8), in0=gath[:, 16 * jj:16 * jj + 8],
            scalar1=cmask[:, jj:jj + 1], scalar2=cmask[:, 3 + jj:4 + jj],
            op0=ALU.mult, op1=ALU.add))
        run("vector", lambda v, jj=jj: v.tensor_scalar(
            out=sS(FB8, FB8 + 8), in0=gath[:, 16 * jj + 8:16 * jj + 16],
            scalar1=cmask[:, jj:jj + 1], scalar2=0.0,
            op0=ALU.mult, op1=ALU.bypass), dep=False)
        run("vector", lambda v, jj=jj: v.tensor_scalar(
            out=sS(FC8, FC8 + 8), in0=tmix[:, W8:W8 + 8],
            scalar1=cmask[:, 6 + jj:7 + jj], scalar2=0.0,
            op0=ALU.mult, op1=ALU.bypass), dep=False)
        run("vector", lambda v: v.tensor_tensor(
            out=sS(FC8, FC8 + 8), in0=sS(FC8, FC8 + 8), in1=sS(SP8, SP8 + 8),
            op=ALU.add))
        run("vector", lambda v: v.tensor_tensor(
            out=sS(FD8, FD8 + 8), in0=sS(FA8, FA8 + 8), in1=sS(FC8, FC8 + 8),
            op=ALU.max))
        run("vector", lambda v: v.tensor_tensor(
            out=sS(FA8, FA8 + 8), in0=sS(FA8, FA8 + 8), in1=sS(FD8, FD8 + 8),
            op=ALU.subtract))
        v_f = inc("vector", "vS", lambda v: v.tensor_tensor(
            out=sS(FC8, FC8 + 8), in0=sS(FC8, FC8 + 8), in1=sS(FD8, FD8 + 8),
            op=ALU.subtract))
        wait("scalar", "vS", v_f)
        run("scalar", lambda s: s.activation(
            out=sS(FA8, FA8 + 8), in_=sS(FA8, FA8 + 8), func=AF.Exp))
        a_f = inc("scalar", "aS", lambda s: s.activation(
            out=sS(FC8, FC8 + 8), in_=sS(FC8, FC8 + 8), func=AF.Exp))
        wait("vector", "aS", a_f)
        run("vector", lambda v: v.tensor_tensor(
            out=sS(FB8, FB8 + 8), in0=sS(FB8, FB8 + 8), in1=sS(FA8, FA8 + 8),
            op=ALU.mult))
        run("vector", lambda v: v.tensor_tensor(
            out=sS(FC8, FC8 + 8), in0=sS(SQ8, SQ8 + 8), in1=sS(FC8, FC8 + 8),
            op=ALU.mult))
        run("vector", lambda v: v.tensor_tensor(
            out=sS(SQ8, SQ8 + 8), in0=sS(FB8, FB8 + 8), in1=sS(FC8, FC8 + 8),
            op=ALU.add))
        run("vector", lambda v: v.tensor_copy(sS(SP8, SP8 + 8), sS(FD8, FD8 + 8)))
    inc("vector", "vS", lambda v: v.tensor_tensor(
        out=sS(PINW8, PINW8 + 8), in0=sS(SP8, SP8 + 8), in1=tmix[:, W8:W8 + 8],
        op=ALU.add))

    attn_u2(0, redo=True)
    attn_kxx_k(0, redo=True)
    attn_kxx_v(0)
    attn_mixr(0)
    attn_head_late(0)
    attn_head_b(0)
    attn_tail(0)
    attn_wo_mm(0)
    attn_wo_res(0)

    wait("sync", "tS", at["t_k_prev"])
    d_h2 = dma("sync", DX, slot[0][:].rearrange("p (b j) -> p b j", b=DT),
               h_in.rearrange("(b p) j -> p b j", p=128))
    wait("sync", "tS", at["t_r_prev"])
    dwait("sync", st["d_wq_stash"])
    d_wq2 = dma("sync", DX, slot[1][:], wqd)
    dwait("tensor", d_h2)
    dwait("tensor", d_wq2)
    st["last_resid"] = at["last_resid"]
    st["last_t1evac"] = at["u2_prev_evac"]
    st["last_t2evac"] = at["u2_prev_evac"]
    st["half_a"] = [at["a_r"][3], at["a_r"][7]]
    st["half_v"] = [at["pE_wo_free"], 0]
    wait("scalar", "tS", at["t_v_prev"])
    wait("scalar", "gS", max(at["v_xr"], at["v_xv"]))
    xts2 = [x2c[1][:, 0:1024], x2c[1][:, 1024:2048],
            x2c[2][:, 0:1024], x2c[2][:, 1024:2048]]
    units2 = list(range(3, 17)) + [1, 2]
    U2 = st["u"]

    def ffn_load2(k):
        unit = units2[k]
        kw = dict(xt_override=xts2[k % 4])
        if k == 0:
            kw.update(prev_u=U2 - 4, extra_dw=[at["d_xh"]],
                      extra_aS=at["x2c_a"][1])
        elif k == 1:
            kw.update(prev_u=U2 - 2)
        elif k in (2, 3):
            kw.update(prev_u=-1, extra_dw=[at["x3_store"][5]],
                      extra_aS=at["x2c_a"][2])
        if unit in (1, 2):
            kw.setdefault("extra_dw", []).append(at["x3_store"][0])
        ffn_load(unit, 2, **kw)

    ffn_load2(0)
    for k in range(16):
        fu = U2 + k
        ffn_t1(fu)
        if k >= 2:
            ffn_resid(fu - 2)
        if k >= 1:
            ffn_quant(fu - 1)
        ffn_hmm(fu)
        if k >= 1:
            ffn_mm2(fu - 1)
        if k + 1 < 16:
            ffn_load2(k + 1)
        if k >= 2:
            ffn_stat2(fu - 2)
        ffn_stats(fu)
    ffn_resid(U2 + 14)
    ffn_quant(U2 + 15)
    ffn_mm2(U2 + 15)
    ffn_stat2(U2 + 14)
    ffn_resid(U2 + 15)
    ffn_stat2(U2 + 15)

    dwait_all("gpsimd")
    dwait_all("sync")

    def _mk(eng_ops, auto_drain):
        def f(e):
            pending = False
            for kind, fn in eng_ops:
                if kind == "o" and auto_drain and pending:
                    e.drain()
                    pending = False
                fn(e)
                if kind in ("o", "i"):
                    pending = True
        return f

    with nc.Block() as blk:
        blk.sync(_mk(ops["sync"], False))
        blk.tensor(_mk(ops["tensor"], False))
        blk.vector(_mk(ops["vector"], True))
        blk.scalar(_mk(ops["scalar"], True))
        blk.gpsimd(_mk(ops["gpsimd"], True))
    return nc


def _hadamard_bf16():
    h = np.array([[1.0]], dtype=np.float64)
    while h.shape[0] < D:
        h = np.block([[h, h], [h, -h]])
    return (h / np.sqrt(D)).astype(np.float32).astype(ml_dtypes.bfloat16)


def kernel(L0_attn_x, L0_attn_time_mix_k, L0_attn_time_mix_v, L0_attn_time_mix_r,
           L0_attn_W_k, L0_attn_W_v, L0_attn_W_r, L0_attn_W_o,
           L0_attn_time_decay, L0_attn_time_first, L0_ffn_weight):
    x = np.ascontiguousarray(np.asarray(L0_attn_x, np.float32))
    Wk = np.ascontiguousarray(np.asarray(L0_attn_W_k, np.float32))
    Wv = np.ascontiguousarray(np.asarray(L0_attn_W_v, np.float32))
    Wr = np.ascontiguousarray(np.asarray(L0_attn_W_r, np.float32))
    Wo = np.ascontiguousarray(np.asarray(L0_attn_W_o, np.float32))
    Wf = np.ascontiguousarray(np.asarray(L0_ffn_weight, np.float32))
    tmk = np.asarray(L0_attn_time_mix_k, np.float32)
    tmv = np.asarray(L0_attn_time_mix_v, np.float32)
    tmr = np.asarray(L0_attn_time_mix_r, np.float32)
    w = np.asarray(L0_attn_time_decay, np.float32)
    u = np.asarray(L0_attn_time_first, np.float32)

    def fm8(vec):
        return np.ascontiguousarray(vec.reshape(DT, 128).T)

    tmix = np.zeros((128, 40), np.float32)
    tmix[:, 0:8] = fm8(tmk)
    tmix[:, 8:16] = fm8(tmv)
    tmix[:, 16:24] = fm8(tmr)
    tmix[:, 24:32] = fm8(w)
    tmix[:, 32:40] = fm8(u)
    Hb = _hadamard_bf16()
    eye32 = np.eye(128, dtype=np.float32)
    eyebf = np.eye(128, dtype=np.float32).astype(ml_dtypes.bfloat16)
    constv = np.zeros((128, 3), np.float32)
    constv[:, 0] = BIGEPS
    constv[:, 1] = LN_D
    constv[:, 2] = EPS_LN

    nc = build_nc()
    in_maps = []
    for c in range(8):
        b, qd = c // 4, c % 4
        lo = qd * TSH
        shard = np.zeros((TSH + 1, D), np.float32)
        shard[1:] = x[b, lo:lo + TSH]
        if qd > 0:
            shard[0] = x[b, lo - 1]
        cm = np.zeros((128, 9), np.float32)
        for jj in range(3):
            m = 1.0 if jj < qd else 0.0
            cm[:, jj] = m
            cm[:, 3 + jj] = (m - 1.0) * 1e30
            cm[:, 6 + jj] = m * TSH
        in_maps.append({
            "x": shard, "wk": Wk, "wv": Wv, "wr": Wr, "wo": Wo, "wf": Wf,
            "hmat": Hb, "tmix": tmix, "cmask": cm, "eye32": eye32, "eyebf": eyebf,
            "constv": constv,
        })
    res = run_bass_kernel_spmd(nc, in_maps, core_ids=list(range(8)))
    out = np.empty((2, 8192, D), np.float32)
    for c, r in enumerate(res.results):
        out[c // 4, (c % 4) * TSH:(c % 4 + 1) * TSH] = r["out"]
    return out


# version 30 (final)
# speedup vs baseline: 1.0580x; 1.0580x over previous
import numpy as np
import ml_dtypes
import concourse.bass as bass
import concourse.mybir as mybir
from concourse.bass_utils import run_bass_kernel_spmd

F32 = mybir.dt.float32
BF16 = mybir.dt.bfloat16
AF = mybir.ActivationFunctionType
ALU = mybir.AluOpType
AX = mybir.AxisListType

D = 1024
DT = 8
TSH = 2048
C = 256
NCH = TSH // C
MAGIC = 12582912.0
NEG = -1e30
EPS_LN = 1e-5
BIGEPS = float(np.float32(D * D * EPS_LN))
LN_D = float(np.float32(np.log(np.float32(D))))

ENGS = ("sync", "tensor", "vector", "scalar", "gpsimd")


def build_nc():
    nc = bass.Bass("TRN2", target_bir_lowering=False)

    x_in = nc.dram_tensor("x", [TSH + 1, D], F32, kind="ExternalInput").ap()
    wk_in = nc.dram_tensor("wk", [D, D], F32, kind="ExternalInput").ap()
    wv_in = nc.dram_tensor("wv", [D, D], F32, kind="ExternalInput").ap()
    wr_in = nc.dram_tensor("wr", [D, D], F32, kind="ExternalInput").ap()
    wo_in = nc.dram_tensor("wo", [D, D], F32, kind="ExternalInput").ap()
    wf_in = nc.dram_tensor("wf", [D, D], F32, kind="ExternalInput").ap()
    h_in = nc.dram_tensor("hmat", [D, D], BF16, kind="ExternalInput").ap()
    tmix_in = nc.dram_tensor("tmix", [128, 40], F32, kind="ExternalInput").ap()
    cmask_in = nc.dram_tensor("cmask", [128, 9], F32, kind="ExternalInput").ap()
    eye32_in = nc.dram_tensor("eye32", [128, 128], F32, kind="ExternalInput").ap()
    eyebf_in = nc.dram_tensor("eyebf", [128, 128], BF16, kind="ExternalInput").ap()
    constv_in = nc.dram_tensor("constv", [128, 3], F32, kind="ExternalInput").ap()
    out_d = nc.dram_tensor("out", [TSH, D], F32, kind="ExternalOutput").ap()

    x2d = nc.dram_tensor("x2d", [TSH + 1, D], F32, kind="Internal").ap()
    x3d = nc.dram_tensor("x3d", [TSH, D], F32, kind="Internal").ap()
    wqd = nc.dram_tensor("wqd", [128, DT * D], BF16, kind="Internal").ap()
    cc_in = nc.dram_tensor("cc_in", [128, 16], F32, kind="Internal")
    cc_out = nc.dram_tensor("cc_out", [512, 16], F32, kind="Internal")

    slot = [nc.alloc_sbuf_tensor(f"wslot{i}", [128, DT * D], BF16) for i in range(4)]
    u2s = [nc.alloc_sbuf_tensor(f"u2s{i}", [128, DT * (C + 1)], F32) for i in range(2)]
    pbuf = nc.alloc_sbuf_tensor("pbuf", [128, DT * (C + 1)], F32)
    qbuf = nc.alloc_sbuf_tensor("qbuf", [128, DT * (C + 1)], F32)
    kbuf = nc.alloc_sbuf_tensor("kbuf", [128, DT * C], F32)
    abuf = nc.alloc_sbuf_tensor("abuf", [128, DT * C], F32)
    vbuf = nc.alloc_sbuf_tensor("vbuf", [128, DT * C], F32)
    rtbuf = nc.alloc_sbuf_tensor("rtbuf", [128, DT * C], F32)
    sc1 = nc.alloc_sbuf_tensor("sc1", [128, DT * C], F32)
    sc2 = nc.alloc_sbuf_tensor("sc2", [128, DT * C], F32)
    sc3 = nc.alloc_sbuf_tensor("sc3", [128, DT * C], F32)
    sc4 = nc.alloc_sbuf_tensor("sc4", [128, DT * C], F32)
    mixb = nc.alloc_sbuf_tensor("mixb", [128, DT * C], BF16)
    mixb2 = nc.alloc_sbuf_tensor("mixb2", [128, DT * C], BF16)
    mixb3 = nc.alloc_sbuf_tensor("mixb3", [128, DT * C], BF16)
    ybuf = nc.alloc_sbuf_tensor("ybuf", [128, DT * C], BF16)
    x2c = [nc.alloc_sbuf_tensor(f"x2c{i}", [128, 2 * D], F32) for i in range(3)]
    u2tm = nc.alloc_sbuf_tensor("u2tm", [128, D], F32)
    tmix = nc.alloc_sbuf_tensor("tmix_s", [128, 40], F32)
    tmixg = nc.alloc_sbuf_tensor("tmixg_s", [128, 40], F32)
    cmask = nc.alloc_sbuf_tensor("cmask_s", [128, 9], F32)
    eye32 = nc.alloc_sbuf_tensor("eye32_s", [128, 128], F32)
    eyebf = nc.alloc_sbuf_tensor("eyebf_s", [128, 128], BF16)
    iota = nc.alloc_sbuf_tensor("iota_s", [128, C], F32)
    rstd2 = nc.alloc_sbuf_tensor("rstd2", [128, 17], F32)
    nmr2 = nc.alloc_sbuf_tensor("nmr2", [128, 17], F32)
    smallS = nc.alloc_sbuf_tensor("smallS", [128, 128], F32)
    bnb = nc.alloc_sbuf_tensor("bnb", [128, 12], F32)
    constv = nc.alloc_sbuf_tensor("constv_s", [128, 3], F32)
    sumb = nc.alloc_sbuf_tensor("sumb", [128, 16], F32)
    gath = nc.alloc_sbuf_tensor("gath", [128, 64], F32)

    pE = nc.alloc_psum_tensor("pE", [128, 2048], F32)
    pA = nc.alloc_psum_tensor("pA", [128, 1024], F32)
    pB = nc.alloc_psum_tensor("pB", [128, 1024], F32)
    pBbf = pB[:].bitcast(BF16)
    pAbf = pA[:].bitcast(BF16)
    nc.const_aps.register(F32, BIGEPS, constv[:, 0:1])
    nc.const_aps.register(F32, LN_D, constv[:, 1:2])
    nc.const_aps.register(F32, EPS_LN, constv[:, 2:3])

    sems, cnt, waited = {}, {}, {}
    DX = ["dXa", "dXb", "dXc", "dXd"]
    DO = ["dOa", "dOb", "dOc", "dOd"]
    for s in DX + DO + ["dC", "tS", "vS", "aS", "gS"]:
        sems[s] = nc.alloc_semaphore(s)
        cnt[s] = 0
    pool_idx = {"dX": 0, "dO": 0}
    ops = {e: [] for e in ENGS}

    def wait(eng, sname, val):
        if val <= 0:
            return
        assert val <= cnt[sname], f"wait ahead of inc: {eng} {sname} {val}>{cnt[sname]}"
        key = (eng, sname)
        if waited.get(key, 0) >= val:
            return
        waited[key] = val
        ops[eng].append(("w", lambda e, s=sems[sname], v=val: e.wait_ge(s, v)))

    def run(eng, fn, dep=True):
        ops[eng].append(("o" if dep else "i", fn))

    def inc(eng, sname, fn, dep=True):
        cnt[sname] += 1
        v = cnt[sname]
        ops[eng].append(("o" if dep else "i",
                         lambda e, f=fn, s=sems[sname]: f(e).then_inc(s, 1)))
        return v

    def dma(eng, pool, out_ap, in_ap):
        pname = "dX" if pool is DX else "dO"
        sname = pool[pool_idx[pname] % len(pool)]
        pool_idx[pname] += 1
        wait(eng, sname, cnt[sname])
        cnt[sname] += 16
        v = cnt[sname]
        ops[eng].append(
            ("d", lambda e, o=out_ap, i=in_ap, s=sems[sname]:
             e.dma_start(o, i).then_inc(s, 16)))
        return (sname, v)

    def dwait(eng, tok):
        if tok:
            wait(eng, tok[0], tok[1])

    def dwait_all(eng):
        for s in DX + DO:
            wait(eng, s, cnt[s])

    def barrier(engs=ENGS, sems_list=("tS", "vS", "aS", "gS")):
        for e in engs:
            for s in sems_list:
                wait(e, s, cnt[s])
            dwait_all(e)

    def v3(t, off, n):
        return t[:].rearrange("p (b c) -> p b c", b=DT)[:, :, off:off + n]

    def r8(t):
        return t[:].rearrange("p (b c) -> p b c", b=DT)

    def bc3(col_ap):
        return col_ap.unsqueeze(2).broadcast_to([128, DT, C])

    SP8, SQ8, FA8, FB8, FC8, FD8, PINW8, EW8 = 64, 72, 80, 88, 96, 104, 112, 120
    W8 = 24
    U8 = 32

    def sS(a, b=None):
        return smallS[:, a:a + 1] if b is None else smallS[:, a:b]

    d_cv = dma("sync", DX, constv[:], constv_in)
    d_tm = dma("sync", DX, tmix[:], tmix_in)
    d_tg = dma("sync", DX, tmixg[:], tmix_in)
    d_cm = dma("sync", DX, cmask[:], cmask_in)
    d_e3 = dma("sync", DX, eye32[:], eye32_in)
    d_eb = dma("sync", DX, eyebf[:], eyebf_in)
    d_h1 = dma("sync", DX, slot[0][:].rearrange("p (b j) -> p b j", b=DT),
               h_in.rearrange("(b p) j -> p b j", p=128))
    const_toks = [d_cv, d_tm, d_tg, d_cm, d_e3, d_eb, d_h1]

    def wait_consts(eng):
        for t in const_toks:
            dwait(eng, t)

    g_iota = inc("gpsimd", "gS", lambda g: g.iota(
        iota[:], pattern=[[1, C]], base=0, channel_multiplier=0,
        allow_small_or_imprecise_dtypes=True))
    dwait("scalar", d_tm)
    a_ew = inc("scalar", "aS", lambda s: s.activation(
        out=sS(EW8, EW8 + 8), in_=tmix[:, W8:W8 + 8], func=AF.Exp))

    def emit_wqprep():
        stg = [kbuf[:, 0:1024], kbuf[:, 1024:2048]]
        vq_by_par = [0, 0]
        for et in range(DT):
            par = et % 2
            wait("sync", "vS", vq_by_par[par])
            d_l = dma("sync", DX, stg[par], wf_in[128 * et:128 * et + 128, :])
            dwait("vector", d_l)
            s = stg[par]
            run("vector", lambda v, s=s: v.tensor_reduce(
                out=sS(58), in_=s, axis=AX.X, op=ALU.add, apply_absolute_value=True))
            run("vector", lambda v: v.tensor_scalar(
                out=sS(59), in0=sS(58), scalar1=1.0 / D, scalar2=EPS_LN,
                op0=ALU.mult, op1=ALU.max))
            run("vector", lambda v: v.tensor_scalar(
                out=sS(60), in0=sS(59), scalar1=0.5, scalar2=0.0,
                op0=ALU.mult, op1=ALU.bypass))
            run("vector", lambda v: v.tensor_scalar(
                out=sS(61), in0=sS(59), scalar1=-0.5, scalar2=0.0,
                op0=ALU.mult, op1=ALU.bypass), dep=False)
            run("vector", lambda v, s=s: v.tensor_scalar(
                out=abuf[:, 0:1024], in0=s, scalar1=sS(60), scalar2=0.0,
                op0=ALU.is_gt, op1=ALU.bypass))
            run("vector", lambda v, s=s: v.tensor_scalar(
                out=abuf[:, 1024:2048], in0=s, scalar1=sS(61), scalar2=0.0,
                op0=ALU.is_lt, op1=ALU.bypass), dep=False)
            run("vector", lambda v: v.tensor_tensor(
                out=vbuf[:, 0:1024], in0=abuf[:, 0:1024], in1=abuf[:, 1024:2048],
                op=ALU.subtract))
            vq_by_par[par] = inc("vector", "vS", lambda v, et=et: v.tensor_scalar(
                out=slot[1][:, et * 1024:(et + 1) * 1024], in0=vbuf[:, 0:1024],
                scalar1=sS(59), scalar2=0.0, op0=ALU.mult, op1=ALU.bypass))
        st["vq_all"] = cnt["vS"]
        wait("tensor", "vS", st["vq_all"])
        wait_consts("tensor")
        aq_half = [0, 0]
        aq = 0
        for et in range(DT):
            hb = et % 2
            if aq_half[hb]:
                wait("tensor", "aS", aq_half[hb])
            tq = 0
            for ib in range(DT):
                tq = inc("tensor", "tS", lambda t, et=et, ib=ib, hb=hb: t.transpose(
                    pBbf[:, hb * 1024 + ib * 128: hb * 1024 + (ib + 1) * 128],
                    in_=slot[1][:, et * 1024 + ib * 128: et * 1024 + ib * 128 + 128],
                    identity=eyebf[:]), dep=False)
            wait("scalar", "tS", tq)
            aq = inc("scalar", "aS", lambda s, et=et, hb=hb: s.activation(
                out=slot[2][:].rearrange("p (b c) -> p b c", b=DT)[
                    :, :, et * 128:(et + 1) * 128],
                in_=pBbf[:, hb * 1024:(hb + 1) * 1024].rearrange(
                    "p (b c) -> p b c", b=DT),
                func=AF.Copy))
            aq_half[hb] = aq
        wait("scalar", "aS", aq)
        st["d_wq_stash"] = dma("scalar", DO, wqd, slot[2][:])

    st = {"last_resid": 0, "last_t1evac": 0, "last_t2evac": 0,
          "last_store": {}, "u": 0, "last_t2_t": 0,
          "half_a": [0, 0], "half_v": [0, 0], "fstash": {}, "s2stash": {},
          "stat_free_a": [0, 0, 0, 0]}

    ffn_xts = [x2c[0][:, 0:1024], x2c[1][:, 0:1024],
               x2c[0][:, 1024:2048], x2c[1][:, 1024:2048]]

    def ffn_load(unit, mode, xt_override=None, prev_u=None, extra_dw=(),
                 extra_aS=0):
        u = st["u"]
        st["u"] += 1
        pu = u % 2
        pu4 = u % 4
        xt = xt_override if xt_override is not None else ffn_xts[pu4]
        xfm = mixb2[:, pu * 1024:(pu + 1) * 1024]
        base = pu4 * 16
        cM, cV, cR1, cM32 = base, base + 1, base + 2, base + 3
        if mode == 1:
            src_ap = x_in[0:128, :] if unit == 0 else \
                x_in[1 + 128 * (unit - 1): 129 + 128 * (unit - 1), :]
        else:
            src_ap = x3d[128 * (unit - 1): 128 * (unit - 1) + 128, :]

        prev = st["last_store"].get(u - 4 if prev_u is None else prev_u)
        if prev is not None:
            dwait("sync", prev[0])
            if prev[1]:
                wait("sync", "vS", prev[1])
        for tok in extra_dw:
            dwait("sync", tok)
        if extra_aS:
            wait("sync", "aS", extra_aS)
        d_x = dma("sync", DX, xt, src_ap)
        st["fstash"][u] = dict(unit=unit, mode=mode, pu=pu, pu4=pu4, base=base,
                               hh=u % 2, xt=xt, xfm=xfm, d_x=d_x,
                               cM=cM, cV=cV, cR1=cR1, cM32=cM32)

    def ffn_stats(u):
        fs = st["fstash"][u]
        pu4, xt = fs["pu4"], fs["xt"]
        cM, cV, cR1, cM32 = fs["cM"], fs["cV"], fs["cR1"], fs["cM32"]
        dwait("vector", fs["d_x"])
        wait("vector", "aS", st["stat_free_a"][pu4])
        run("vector", lambda v, xt=xt: v.bn_stats(out=bnb[:, 0:6], in_=xt[:, 0:512]))
        run("vector", lambda v, xt=xt: v.bn_stats(out=bnb[:, 6:12],
                                                  in_=xt[:, 512:1024]), dep=False)
        v_agg = inc("vector", "vS", lambda v: v.bn_aggr(
            out=sS(cM, cM + 2), in_=bnb[:]))
        wait("scalar", "vS", v_agg)
        run("scalar", lambda s: s.activation(
            out=sS(cR1), in_=sS(cV), func=AF.Ln, bias=EPS_LN))
        a_rstd1 = inc("scalar", "aS", lambda s: s.activation(
            out=sS(cR1), in_=sS(cR1), func=AF.Exp, scale=-0.5))
        a_m32 = inc("scalar", "aS", lambda s: s.activation(
            out=sS(cM32), in_=sS(cM), func=AF.Identity, scale=32.0), dep=False)
        fs["a_rstd1"], fs["a_m32"] = a_rstd1, a_m32

    def ffn_t1(u):
        fs = st["fstash"][u]
        xt, xfm = fs["xt"], fs["xfm"]
        dwait("tensor", fs["d_x"])
        wait("tensor", "aS", max(st["last_t1evac"], st["last_t2evac"]))
        t_t1 = 0
        for b in range(DT):
            t_t1 = inc("tensor", "tS", lambda t, xt=xt, b=b: t.transpose(
                pA[:, b * 128:(b + 1) * 128], in_=xt[:, b * 128:(b + 1) * 128],
                identity=eye32[:]), dep=False)
        wait("scalar", "tS", t_t1)
        a_t1 = inc("scalar", "aS", lambda s, xfm=xfm: s.activation(
            out=xfm, in_=pA[:], func=AF.Copy))
        st["last_t1evac"] = a_t1
        fs["a_t1"] = a_t1

    def ffn_hmm(u):
        fs = st["fstash"][u]
        hh, xfm = fs["hh"], fs["xfm"]
        wait("tensor", "aS", max(fs["a_t1"], st["half_a"][hh]))
        wait("tensor", "vS", st["half_v"][hh])
        for jc in range(2):
            for db in range(DT):
                inc("tensor", "tS", lambda t, xfm=xfm, jc=jc, db=db, hh=hh: t.matmul(
                    pE[:, hh * 1024 + jc * 512: hh * 1024 + (jc + 1) * 512],
                    lhsT=xfm[:, db * 128:(db + 1) * 128],
                    rhs=slot[0][:, db * 1024 + jc * 512: db * 1024 + jc * 512 + 512],
                    start=(db == 0), stop=(db == DT - 1)), dep=False)
        fs["t_h"] = cnt["tS"]

    def ffn_quant(u):
        fs = st["fstash"][u]
        pu, hh = fs["pu"], fs["hh"]
        base = fs["base"]
        cM32, cR1 = fs["cM32"], fs["cR1"]
        cMX, cT, cRT, cCC, cCS, cAB = (base + 4, base + 5, base + 6, base + 7,
                                       base + 8, base + 9)
        zb = sc1[:, pu * 1024:(pu + 1) * 1024]
        q1 = sc2[:, pu * 1024:(pu + 1) * 1024]
        Qb = mixb[:, 0:1024]
        hE = pE[:, hh * 1024:(hh + 1) * 1024]

        wait("vector", "tS", max(fs["t_h"], st["last_t2_t"]))
        wait("vector", "aS", fs["a_m32"])
        run("vector", lambda v, hE=hE: v.tensor_reduce(
            out=sS(cMX), in_=hE[:, 1:1024], axis=AX.X, op=ALU.max,
            apply_absolute_value=True), dep=False)
        run("vector", lambda v, hE=hE: v.tensor_scalar(
            out=hE[:, 0:1], in0=hE[:, 0:1], scalar1=sS(cM32), scalar2=0.0,
            op0=ALU.subtract, op1=ALU.bypass), dep=False)
        run("vector", lambda v, hE=hE: v.tensor_scalar(
            out=sS(cAB), in0=hE[:, 0:1], scalar1=-1.0, scalar2=0.0,
            op0=ALU.mult, op1=ALU.max))
        run("vector", lambda v: v.tensor_tensor(
            out=sS(cMX), in0=sS(cMX), in1=sS(cAB), op=ALU.max))
        wait("vector", "aS", fs["a_rstd1"])
        run("vector", lambda v: v.tensor_scalar(
            out=sS(cT), in0=sS(cMX), scalar1=sS(cR1), scalar2=EPS_LN,
            op0=ALU.mult, op1=ALU.max))
        run("vector", lambda v: v.reciprocal(out=sS(cRT), in_=sS(cT)))
        run("vector", lambda v: v.tensor_scalar(
            out=sS(cCC), in0=sS(cRT), scalar1=sS(cR1), scalar2=7.0,
            op0=ALU.mult, op1=ALU.mult))
        v_coef = inc("vector", "vS", lambda v: v.tensor_scalar(
            out=sS(cCS), in0=sS(cT), scalar1=0.5 / 7.0, scalar2=0.0,
            op0=ALU.mult, op1=ALU.bypass), dep=False)
        wait("scalar", "vS", v_coef)
        a_z = inc("scalar", "aS", lambda s, zb=zb, hE=hE: s.activation(
            out=zb, in_=hE, func=AF.Copy, scale=sS(cCC), bias=MAGIC))
        st["half_a"][hh] = a_z
        st["half_v"][hh] = v_coef
        wait("vector", "aS", a_z)
        run("vector", lambda v, zb=zb, q1=q1: v.tensor_scalar(
            out=q1, in0=zb, scalar1=MAGIC, scalar2=-8.0,
            op0=ALU.subtract, op1=ALU.max), dep=False)
        v_q = inc("vector", "vS", lambda v, q1=q1: v.tensor_scalar(
            out=Qb, in0=q1, scalar1=7.0, scalar2=sS(cCS),
            op0=ALU.min, op1=ALU.mult))
        pu4 = fs["pu4"]
        st["stat_free_a"][pu4] = max(st["stat_free_a"][pu4], fs["a_m32"])
        fs["v_q"], fs["a_z"] = v_q, a_z

    def ffn_mm2(u):
        fs = st["fstash"][u]
        unit, mode, pu, hh, xt = fs["unit"], fs["mode"], fs["pu"], fs["hh"], fs["xt"]
        cM, cV = fs["cM"], fs["cV"]
        Qb = mixb[:, 0:1024]
        Qfm = mixb[:, 1024:2048]
        if mode == 1:
            dst = x2d[0:128, :] if unit == 0 else \
                x2d[1 + 128 * (unit - 1): 129 + 128 * (unit - 1), :]
            stat_col = unit
            wslot_q = 2
        else:
            dst = out_d[128 * (unit - 1): 128 * (unit - 1) + 128, :]
            stat_col = None
            wslot_q = 1
        wait("tensor", "vS", fs["v_q"])
        wait("tensor", "aS", st["last_t1evac"])
        t_t2 = 0
        for b in range(DT):
            t_t2 = inc("tensor", "tS", lambda t, b=b: t.transpose(
                pAbf[:, b * 128:(b + 1) * 128],
                in_=Qb[:, b * 128:(b + 1) * 128], identity=eyebf[:]), dep=False)
        st["last_t2_t"] = t_t2
        wait("scalar", "tS", t_t2)
        a_t2 = inc("scalar", "aS", lambda s: s.activation(
            out=Qfm, in_=pAbf[:, 0:1024], func=AF.Copy))
        st["last_t2evac"] = a_t2
        wait("tensor", "aS", a_t2)
        wait("tensor", "vS", st["last_resid"])
        t_w = 0
        for oc in range(2):
            for ib in range(DT):
                t_w = inc("tensor", "tS", lambda t, oc=oc, ib=ib, ws=wslot_q: t.matmul(
                    pB[:, oc * 512:(oc + 1) * 512],
                    lhsT=Qfm[:, ib * 128:(ib + 1) * 128],
                    rhs=slot[ws][:, ib * 1024 + oc * 512: ib * 1024 + oc * 512 + 512],
                    start=(ib == 0), stop=(ib == DT - 1)), dep=False)
        fs["t_w"] = t_w
        fs["dst"] = dst

    def ffn_resid(u):
        fs = st["fstash"].pop(u)
        unit, mode, xt, dst = fs["unit"], fs["mode"], fs["xt"], fs["dst"]
        wait("vector", "tS", fs["t_w"])
        v_res = inc("vector", "vS", lambda v, xt=xt: v.tensor_tensor(
            out=xt, in0=xt, in1=pB[:], op=ALU.add), dep=False)
        st["last_resid"] = v_res
        wait("scalar", "vS", v_res)
        if mode == 1 and unit == 0:
            d_s = dma("scalar", DO, x2d[0:1, :], xt[0:1, :])
        else:
            d_s = dma("scalar", DO, dst, xt)
        st["last_store"][u] = (d_s, 0)
        fs["d_s"] = d_s
        st["s2stash"][u] = fs

    def ffn_stat2(u):
        fs = st["s2stash"].pop(u)
        if fs["mode"] != 1:
            return
        pu4, xt, unit = fs["pu4"], fs["xt"], fs["unit"]
        cM, cV = fs["cM"], fs["cV"]
        stat_col = unit
        run("vector", lambda v, xt=xt: v.bn_stats(out=bnb[:, 0:6],
                                                  in_=xt[:, 0:512]))
        run("vector", lambda v, xt=xt: v.bn_stats(out=bnb[:, 6:12],
                                                  in_=xt[:, 512:1024]), dep=False)
        v_agg2 = inc("vector", "vS", lambda v: v.bn_aggr(
            out=sS(cM, cM + 2), in_=bnb[:]))
        wait("scalar", "vS", v_agg2)
        run("scalar", lambda s, sc=stat_col: s.activation(
            out=rstd2[:, sc:sc + 1], in_=sS(cV), func=AF.Ln, bias=EPS_LN))
        a_r2 = inc("scalar", "aS", lambda s, sc=stat_col: s.activation(
            out=rstd2[:, sc:sc + 1], in_=rstd2[:, sc:sc + 1],
            func=AF.Exp, scale=-0.5))
        wait("vector", "aS", a_r2)
        v_stat2 = inc("vector", "vS", lambda v, sc=stat_col: v.tensor_scalar(
            out=nmr2[:, sc:sc + 1], in0=sS(cM),
            scalar1=rstd2[:, sc:sc + 1], scalar2=-1.0,
            op0=ALU.mult, op1=ALU.mult), dep=False)
        st["stat_free_a"][pu4] = max(st["stat_free_a"][pu4], a_r2)
        st["last_store"][u] = (fs["d_s"], v_stat2)

    at = {"u2_prev_evac": 0, "x3_store": {}, "last_revac_et": [0] * DT,
          "last_resid": 0, "u2_free_g": {0: 0, 1: 0}, "u2_free_v": {0: 0, 1: 0},
          "last_u2t": 0, "x2c_a": {0: 0, 1: 0, 2: 0}, "mslot": 0,
          "tail_v": {}, "u2_ev": {}, "wo_stash": {}, "wores": {},
          "w_ready": {},
          "abuf_free_v": 0}


    def emit_proj(ws, wait_g, prev_evacs, mb):
        tks = []
        wait("tensor", "aS", at["w_ready"].get(ws, 0))
        wait("tensor", "gS", wait_g)
        for et in range(DT):
            wait("tensor", "aS", prev_evacs[et])
            tm = 0
            for db in range(DT):
                tm = inc("tensor", "tS", lambda t, et=et, db=db, ws=ws, mb=mb:
                         t.matmul(
                    pE[:, et * C:(et + 1) * C],
                    lhsT=slot[ws][:, db * 1024 + et * 128: db * 1024 + et * 128 + 128],
                    rhs=mb[:, db * C:(db + 1) * C],
                    start=(db == 0), stop=(db == DT - 1)), dep=False)
            tks.append(tm)
        return tks

    def emit_evacs(tks, out_t, func, scale=1.0):
        evs = []
        for ep in range(DT // 2):
            wait("scalar", "tS", tks[2 * ep + 1])
            e = inc("scalar", "aS", lambda s, ep=ep, o=out_t, f=func,
                    sc=scale: s.activation(
                out=o[:, ep * 2 * C:(ep + 1) * 2 * C],
                in_=pE[:, ep * 2 * C:(ep + 1) * 2 * C],
                func=f, scale=sc), dep=False)
            evs.extend([e, e])
        return evs

    def emit_mix8(base_col, mb, u2c, wait_t):
        if wait_t:
            wait("gpsimd", "tS", wait_t)
        run("gpsimd", lambda g, bc=base_col, mb=mb: g.tensor_tensor(
            out=r8(mb), in0=r8(sc1), in1=bc3(tmixg[:, bc:bc + 8]), op=ALU.mult))
        gm = inc("gpsimd", "gS", lambda g, mb=mb, u2c=u2c: g.tensor_tensor(
            out=r8(mb), in0=r8(mb), in1=v3(u2c, 0, C), op=ALU.add))
        return gm

    def attn_u2(j, redo=False):
        pu = 0 if redo else j % 2
        bu = 0 if redo else j % 3
        u2c = u2s[pu]
        u2p = u2s[1 - pu]
        first = (j == 0)

        def dw_store(u, full=False):
            p = st["last_store"].get(u)
            if p is not None:
                dwait("sync", p[0])
                if full and p[1]:
                    wait("sync", "vS", p[1])

        if redo:
            dwait("sync", at["x3_store"][NCH - 2])
            dwait("sync", at["x3_store"][NCH - 1])
        else:
            xs = at["x3_store"].get(j - 3)
            if xs is not None:
                dwait("sync", xs)
            dw_store(2 * j + 1)
            dw_store(2 * j + 2)
            if j == 0:
                dw_store(14, full=True)
                dw_store(16, full=True)
                dw_store(13, full=True)
                dw_store(0)
            if j == 1:
                dw_store(13, full=True)
                dw_store(15, full=True)
        wait("sync", "aS", at["x2c_a"][bu])
        if first:
            wait("sync", "aS", at["x2c_a"][1])
        d_x2 = dma(
            "sync", DX,
            x2c[bu][:].rearrange("p (a d) -> p a d", a=2),
            x2d[1 + C * j: 1 + C * (j + 1), :].rearrange("(a p) d -> p a d", p=128))
        d_xh = None
        if first:
            d_xh = dma("sync", DX, x2c[1][:, 0:1024], x2d[0:128, :])
            at["d_xh"] = d_xh

        def u2_build(src_ap, stat_col, off, d_dep, width=128):
            dwait("scalar", d_dep)
            wait("scalar", "gS", at["u2_free_g"][pu])
            wait("scalar", "vS", at["u2_free_v"][pu])
            wait("scalar", "tS", at["last_u2t"])
            a_ap = inc("scalar", "aS", lambda s, src=src_ap, sc=stat_col: s.activation(
                out=u2tm[:], in_=src, func=AF.Identity,
                scale=rstd2[:, sc:sc + 1], bias=nmr2[:, sc:sc + 1]))
            wait("tensor", "aS", max(a_ap, at["u2_prev_evac"]))
            tt = 0
            for b in range(DT):
                tt = inc("tensor", "tS", lambda t, b=b: t.transpose(
                    pA[:, b * 128:(b + 1) * 128], in_=u2tm[:, b * 128:(b + 1) * 128],
                    identity=eye32[:]), dep=False)
            at["last_u2t"] = tt
            wait("scalar", "tS", tt)
            ev = inc("scalar", "aS", lambda s, off=off, w=width, u2c=u2c: s.activation(
                out=v3(u2c, off, w),
                in_=pA[:].rearrange("p (b c) -> p b c", b=DT)[:, :, 0:w],
                func=AF.Copy))
            at["u2_prev_evac"] = ev
            return ev

        if first:
            ev = u2_build(x2c[1][:, 0:1024], 0, 0, d_xh, width=1)
            at["x2c_a"][1] = ev
            ev = u2_build(x2c[bu][:, 0:1024], 1, 1, d_x2)
            ev = u2_build(x2c[bu][:, 1024:2048], 2, 129, d_x2)
            at["x2c_a"][bu] = ev
        else:
            wait("gpsimd", "aS", at["u2_prev_evac"])
            wait("gpsimd", "vS", max(at["u2_free_v"][1 - pu],
                                     at["u2_free_v"][pu]))
            wait("gpsimd", "gS", at["u2_free_g"][pu])
            g_h = inc("gpsimd", "gS", lambda g, u2c=u2c, u2p=u2p: g.tensor_copy(
                v3(u2c, 0, 1), v3(u2p, C, 1)))
            at["u2_free_g"][1 - pu] = g_h
            at["halo_g"] = g_h
            ev = u2_build(x2c[bu][:, 0:1024], 2 * j + 1, 1, d_x2)
            ev = u2_build(x2c[bu][:, 1024:2048], 2 * j + 2, 129, d_x2)
            at["x2c_a"][bu] = ev
        u2cv = u2c[:].rearrange("p (b c) -> p b c", b=DT)
        wait_consts("gpsimd")
        wait("gpsimd", "aS", ev)
        wait("gpsimd", "vS", at.get("sc1_dve_free", 0))
        inc("gpsimd", "gS", lambda g, u2cv=u2cv: g.tensor_tensor(
            out=r8(sc1), in0=u2cv[:, :, 1:1 + C], in1=u2cv[:, :, 0:C],
            op=ALU.subtract))
        v_xk = emit_mix8(0, mixb, u2c,
                         max(at.get("t_k_prev", 0), at.get("mix_pe_free", 0)))
        v_xv = emit_mix8(8, mixb2, u2c,
                         max(at.get("t_v_prev", 0), at.get("mix_pe_free", 0)))
        at["u2_ev"][j] = (ev, pu, v_xk, v_xv)

    def attn_kxx_k(j, redo=False):
        ev, pu, v_xk, v_xv = at["u2_ev"].pop(j)
        u2c = u2s[pu]
        at["u2c"] = u2c
        at["pu"] = pu
        at["v_xv"] = v_xv
        wait("tensor", "vS", at.get("pE_wo_free", 0))
        t_k = emit_proj(0, v_xk, at["last_revac_et"], mixb)
        at["t_k_prev"] = t_k[DT - 1]
        wait("scalar", "gS", at.get("g_a", 0))
        a_k = emit_evacs(t_k, kbuf, AF.Copy)
        at["t_k"], at["a_k"] = t_k, a_k
        at["first"] = (j == 0)
        at["redo"] = redo
        at["j"] = j

    def attn_kxx_v(j):
        a_k = at["a_k"]
        v_xv = at["v_xv"]
        wait("gpsimd", "vS", max(at.get("tail_vg", 0), at.get("v_qs", 0)))
        if j == 0:
            g_pb = inc("gpsimd", "gS", lambda g: g.memset(v3(pbuf, 0, 1), NEG))
            wait("gpsimd", "vS", at.get("tail_vn1", 0))
            g_qb = inc("gpsimd", "gS", lambda g: g.memset(v3(qbuf, 0, 1), 0.0))
        else:
            g_pb = inc("gpsimd", "gS", lambda g: g.tensor_copy(
                v3(pbuf, 0, 1), v3(pbuf, C, 1)))
            wait("gpsimd", "vS", at.get("tail_vn1", 0))
            g_qb = inc("gpsimd", "gS", lambda g: g.tensor_copy(
                v3(qbuf, 0, 1), v3(qbuf, C, 1)))
        at["g_pb"], at["g_qb"] = g_pb, g_qb
        if not (j == 0 and not at["redo"]):
            wait("gpsimd", "tS", at.get("wprep_t", 0))
            wait("gpsimd", "aS", a_k[DT - 1])
            wait("gpsimd", "vS", at["abuf_free_v"])
            g_a = inc("gpsimd", "gS", lambda g: g.tensor_tensor(
                out=r8(abuf), in0=r8(kbuf), in1=bc3(tmixg[:, U8:U8 + 8]),
                op=ALU.add))
            at["g_a"] = g_a
        t_v = emit_proj(2, v_xv, a_k, mixb2)
        at["t_v_prev"] = t_v[DT - 1]
        wait("scalar", "vS", at.get("vbuf_free", 0))
        a_v = emit_evacs(t_v, vbuf, AF.Copy)
        at["t_v"], at["a_v"] = t_v, a_v

    def attn_mixr(j):
        at["v_xr"] = emit_mix8(16, mixb3, at["u2c"], at.get("t_r_prev", 0))

    def attn_head_late(j):
        first, redo = at["first"], at["redo"]
        t_k, a_k = at["t_k"], at["a_k"]
        pu = at["pu"]
        u2c = at["u2c"]

        wait("vector", "gS", at["g_pb"])
        for b in range(DT):
            wait("vector", "aS", a_k[b])
            inc("vector", "vS", lambda v, b=b: v.tensor_tensor_scan(
                out=pbuf[:, b * (C + 1) + 1: b * (C + 1) + 1 + C],
                data0=tmix[:, W8 + b:W8 + b + 1].broadcast_to([128, C]),
                data1=kbuf[:, b * C:(b + 1) * C],
                initial=pbuf[:, b * (C + 1): b * (C + 1) + 1],
                op0=ALU.add, op1=ALU.max), dep=(b == 0))
        t_v, a_v = at["t_v"], at["a_v"]
        run("vector", lambda v: v.tensor_tensor(
            out=r8(sc2), in0=v3(pbuf, 0, C), in1=bc3(tmix[:, W8:W8 + 8]),
            op=ALU.add))
        v_al0 = inc("vector", "vS", lambda v: v.tensor_tensor(
            out=r8(sc3), in0=r8(sc2), in1=v3(pbuf, 1, C), op=ALU.subtract))
        wait("scalar", "vS", v_al0)
        a_ea = inc("scalar", "aS", lambda s: s.activation(
            out=sc3[:], in_=sc3[:], func=AF.Exp))
        v_be = inc("vector", "vS", lambda v: v.tensor_tensor(
            out=r8(sc2), in0=r8(kbuf), in1=v3(pbuf, 1, C), op=ALU.subtract))
        wait("scalar", "vS", v_be)
        a_be = inc("scalar", "aS", lambda s: s.activation(
            out=sc2[:], in_=sc2[:], func=AF.Exp))
        wait("vector", "aS", max(a_be, a_v[DT - 1]))
        v_bv = inc("vector", "vS", lambda v: v.tensor_tensor(
            out=sc2[:], in0=sc2[:], in1=vbuf[:], op=ALU.mult), dep=False)
        at["vbuf_free"] = max(at.get("vbuf_free", 0), v_bv)
        wait("vector", "aS", a_ea)
        wait("vector", "gS", at["g_qb"])
        v_qs = 0
        for b in range(DT):
            v_qs = inc("vector", "vS", lambda v, b=b: v.tensor_tensor_scan(
                out=qbuf[:, b * (C + 1) + 1: b * (C + 1) + 1 + C],
                data0=sc3[:, b * C:(b + 1) * C],
                data1=sc2[:, b * C:(b + 1) * C],
                initial=qbuf[:, b * (C + 1): b * (C + 1) + 1],
                op0=ALU.mult, op1=ALU.add), dep=(b == 0))
        at["v_qs"] = v_qs

    def attn_head_b(j):
        redo = at["redo"]
        pu = at["pu"]
        a_v = at["a_v"]
        t_r = emit_proj(1, at["v_xr"], a_v, mixb3)
        at["t_r_prev"] = t_r[DT - 1]
        wait("scalar", "vS", at.get("last_tail_v", 0))
        a_r = emit_evacs(t_r, rtbuf, AF.Copy)
        at["last_revac_et"] = a_r
        at["a_r"] = a_r
        at["u2_free_g"][pu] = max(at["u2_free_g"][pu], at["v_xr"])
        wait("vector", "aS", a_r[DT - 1])
        v_hr = inc("vector", "vS", lambda v: v.tensor_scalar(
            out=rtbuf[:], in0=rtbuf[:], scalar1=-1.0, scalar2=30.0,
            op0=ALU.mult, op1=ALU.min))
        wait("scalar", "vS", v_hr)
        at["a_er"] = inc("scalar", "aS", lambda s: s.activation(
            out=rtbuf[:], in_=rtbuf[:], func=AF.Exp))

        if redo:
            wait("vector", "gS", max(g_iota, at["v_xr"]))
            run("vector", lambda v: v.tensor_tensor(
                out=r8(sc1), in0=iota[:].unsqueeze(1).broadcast_to([128, DT, C]),
                in1=bc3(tmix[:, W8:W8 + 8]), op=ALU.mult))
            run("vector", lambda v: v.tensor_tensor(
                out=r8(sc1), in0=r8(sc1), in1=bc3(sS(PINW8, PINW8 + 8)),
                op=ALU.add))
            run("vector", lambda v: v.tensor_tensor(
                out=r8(sc2), in0=r8(sc1), in1=v3(pbuf, 1, C), op=ALU.subtract))
            v_pc = inc("vector", "vS", lambda v: v.tensor_tensor(
                out=v3(pbuf, 1, C), in0=v3(pbuf, 1, C), in1=r8(sc1), op=ALU.max))
            wait("scalar", "vS", v_pc)
            run("scalar", lambda s: s.activation(out=sc3[:], in_=sc2[:], func=AF.Relu))
            run("scalar", lambda s: s.activation(
                out=sc3[:], in_=sc3[:], func=AF.Exp, scale=-1.0))
            run("scalar", lambda s: s.activation(
                out=sc2[:], in_=sc2[:], func=AF.Relu, scale=-1.0))
            a_ed = inc("scalar", "aS", lambda s: s.activation(
                out=sc2[:], in_=sc2[:], func=AF.Exp, scale=-1.0))
            wait("vector", "aS", a_ed)
            run("vector", lambda v: v.tensor_tensor(
                out=r8(sc3), in0=r8(sc3), in1=v3(qbuf, 1, C), op=ALU.mult),
                dep=False)
            run("vector", lambda v: v.tensor_tensor(
                out=r8(sc2), in0=r8(sc2), in1=bc3(sS(SQ8, SQ8 + 8)), op=ALU.mult),
                dep=False)
            run("vector", lambda v: v.tensor_tensor(
                out=v3(qbuf, 1, C), in0=r8(sc2), in1=r8(sc3), op=ALU.add))
            run("vector", lambda v: v.tensor_copy(v3(pbuf, 0, 1), sS(SP8, SP8 + 8)))
            run("vector", lambda v: v.tensor_copy(
                v3(qbuf, 0, 1), sS(SQ8, SQ8 + 8)), dep=False)

    def attn_tail(j):
        bu = 0 if at["redo"] and at["j"] == j else j % 3
        g_a = at["g_a"]
        a_r = at["a_r"]
        wait("vector", "gS", g_a)
        v_g = inc("vector", "vS", lambda v: v.tensor_tensor(
            out=r8(sc4), in0=r8(abuf), in1=v3(pbuf, 0, C), op=ALU.subtract))
        at["tail_vg"] = v_g
        run("vector", lambda v: v.tensor_scalar(
            out=sc3[:], in0=sc4[:], scalar1=0.0, scalar2=0.0,
            op0=ALU.max, op1=ALU.bypass))
        v_gn = inc("vector", "vS", lambda v: v.tensor_scalar(
            out=sc4[:], in0=sc4[:], scalar1=0.0, scalar2=0.0,
            op0=ALU.min, op1=ALU.bypass))
        wait("scalar", "vS", v_gn)
        run("scalar", lambda s: s.activation(
            out=sc3[:], in_=sc3[:], func=AF.Exp, scale=-1.0))
        a_esc = inc("scalar", "aS", lambda s: s.activation(
            out=sc4[:], in_=sc4[:], func=AF.Exp))
        wait("vector", "aS", a_esc)
        v_den = inc("vector", "vS", lambda v: v.scalar_tensor_tensor(
            out=abuf[:], in0=sc3[:], scalar=1e-9, in1=sc4[:],
            op0=ALU.add, op1=ALU.add), dep=False)
        wait("vector", "aS", at["a_er"])
        v_d2 = inc("vector", "vS", lambda v: v.scalar_tensor_tensor(
            out=sc2[:], in0=rtbuf[:], scalar=1.0, in1=abuf[:],
            op0=ALU.add, op1=ALU.mult))
        at["abuf_free_v"] = v_d2
        wait("scalar", "vS", v_d2)
        run("scalar", lambda s: s.activation(out=sc2[:], in_=sc2[:], func=AF.Ln))
        a_rd = inc("scalar", "aS", lambda s: s.activation(
            out=sc2[:], in_=sc2[:], func=AF.Exp, scale=-1.0))
        v_n1 = inc("vector", "vS", lambda v: v.tensor_tensor(
            out=r8(sc3), in0=r8(sc3), in1=v3(qbuf, 0, C), op=ALU.mult),
            dep=False)
        at["tail_vn1"] = v_n1
        run("vector", lambda v: v.tensor_tensor(
            out=sc4[:], in0=sc4[:], in1=vbuf[:], op=ALU.mult), dep=False)
        v_num = inc("vector", "vS", lambda v: v.tensor_tensor(
            out=sc3[:], in0=sc3[:], in1=sc4[:], op=ALU.add))
        at["vbuf_free"] = max(at.get("vbuf_free", 0), v_num)
        wait("vector", "aS", a_rd)
        wait("vector", "tS", at.get("wo_t1", 0))
        v_y = inc("vector", "vS", lambda v: v.tensor_tensor(
            out=ybuf[:], in0=sc3[:], in1=sc2[:], op=ALU.mult))
        at["tail_v"][j] = v_y
        at["last_tail_v"] = v_y
        at["wo_stash"][j] = (v_y, bu)

    def attn_wo_mm(j):
        v_y, bu = at["wo_stash"].pop(j)
        wait("tensor", "aS", at["w_ready"].get(3, 0))
        wait("tensor", "vS", v_y)
        wait("tensor", "vS", at["last_resid"])
        wait("tensor", "aS", at["a_r"][3])
        t0 = 0
        for oc in range(2):
            for db in range(DT):
                t0 = inc("tensor", "tS", lambda t, oc=oc, db=db: t.matmul(
                    pE[:, oc * 512:(oc + 1) * 512],
                    lhsT=ybuf[:, db * C: db * C + 128],
                    rhs=slot[3][:, db * 1024 + oc * 512: db * 1024 + oc * 512 + 512],
                    start=(db == 0), stop=(db == DT - 1)), dep=False)
        t1 = 0
        for oc in range(2):
            for db in range(DT):
                t1 = inc("tensor", "tS", lambda t, oc=oc, db=db: t.matmul(
                    pB[:, oc * 512:(oc + 1) * 512],
                    lhsT=ybuf[:, db * C + 128: db * C + 256],
                    rhs=slot[3][:, db * 1024 + oc * 512: db * 1024 + oc * 512 + 512],
                    start=(db == 0), stop=(db == DT - 1)), dep=False)
        at["wores"][j] = (bu, t0, t1)
        at["wo_t1"] = t1

    def attn_wo_res(j):
        bu, t0, t1 = at["wores"].pop(j)
        wait("vector", "tS", t1)
        v_r2a = inc("vector", "vS", lambda v, bu=bu: v.tensor_tensor(
            out=x2c[bu][:, 0:1024], in0=x2c[bu][:, 0:1024],
            in1=pE[:, 0:1024], op=ALU.add))
        v_r2b = inc("vector", "vS", lambda v, bu=bu: v.tensor_tensor(
            out=x2c[bu][:, 1024:2048], in0=x2c[bu][:, 1024:2048],
            in1=pB[:], op=ALU.add), dep=False)
        at["pE_wo_free"] = v_r2a
        at["last_resid"] = v_r2b
        wait("scalar", "vS", v_r2b)
        d_x3 = dma(
            "scalar", DO,
            x3d[C * j: C * (j + 1), :].rearrange("(a p) d -> p a d", p=128),
            x2c[bu][:].rearrange("p (a d) -> p a d", a=2))
        at["x3_store"][j] = d_x3

    dwait("scalar", d_cv)
    wait_consts("vector")
    wait_consts("scalar")
    wait_consts("tensor")
    U0 = st["u"]
    ffn_load(0, mode=1)
    ffn_t1(U0)
    ffn_hmm(U0)
    ffn_load(1, mode=1)
    ffn_stats(U0)
    ffn_t1(U0 + 1)
    ffn_hmm(U0 + 1)
    ffn_load(2, mode=1)
    ffn_stats(U0 + 1)
    ffn_load(3, mode=1)
    emit_wqprep()
    ffn_quant(U0)
    ffn_mm2(U0)
    for k in range(2, 17):
        fu = U0 + k
        ffn_t1(fu)
        ffn_resid(fu - 2)
        ffn_quant(fu - 1)
        ffn_hmm(fu)
        ffn_mm2(fu - 1)
        ffn_stat2(fu - 2)
        if k + 2 < 17:
            ffn_load(k + 2, mode=1)
        ffn_stats(fu)
    ffn_resid(U0 + 15)
    ffn_quant(U0 + 16)
    ffn_mm2(U0 + 16)
    ffn_stat2(U0 + 15)
    ffn_resid(U0 + 16)
    ffn_stat2(U0 + 16)

    wp_t_par = [0, 0]
    wp_last_a = [0]

    def emit_wprep(src, dst_slot, scale, evac_wait_t=0, evac_wait_dO=0):
        stg2 = [abuf[:, 0:1024], abuf[:, 1024:2048]]
        t_by_par = wp_t_par
        wait("sync", "vS", st["vq_all"])
        wait("tensor", "vS", st["last_resid"])
        for et in range(DT):
            par = et % 2
            wait("sync", "tS", t_by_par[par])
            d_l = dma("sync", DX, stg2[par], src[128 * et:128 * et + 128, :])
            dwait("tensor", d_l)
            for half in range(2):
                if wp_last_a[0]:
                    wait("tensor", "aS", wp_last_a[0])
                tq2 = 0
                for b in range(4):
                    ib = half * 4 + b
                    tq2 = inc("tensor", "tS", lambda t, ib=ib, b=b, par=par:
                              t.transpose(pB[:, b * 128:(b + 1) * 128],
                                          in_=stg2[par][:, ib * 128:(ib + 1) * 128],
                                          identity=eye32[:]), dep=False)
                t_by_par[par] = tq2
                wait("scalar", "tS", tq2)
                if evac_wait_t:
                    wait("scalar", "tS", evac_wait_t)
                if evac_wait_dO:
                    dwait("scalar", evac_wait_dO)
                wp_last_a[0] = inc("scalar", "aS", lambda s, et=et, half=half,
                                   d=dst_slot, sc=scale: s.activation(
                    out=slot[d][:].rearrange("p (b c) -> p b c", b=DT)[
                        :, half * 4:half * 4 + 4, et * 128:(et + 1) * 128],
                    in_=pB[:, 0:512].rearrange("p (b c) -> p b c", b=4),
                    func=AF.Copy, scale=sc))
        return wp_last_a[0]

    ffn1_last_t = cnt["tS"]
    at["sc1_dve_free"] = cnt["vS"]
    at["mix_pe_free"] = cnt["tS"]
    attn_u2(0)
    at["w_ready"][0] = emit_wprep(wk_in, 0, 1.0, evac_wait_t=ffn1_last_t)
    attn_kxx_k(0)
    at["w_ready"][2] = emit_wprep(wv_in, 2, 1.0, evac_wait_dO=st["d_wq_stash"])
    attn_kxx_v(0)
    at["w_ready"][1] = emit_wprep(wr_in, 1, 1.0)
    attn_mixr(0)
    attn_u2(1)
    at["w_ready"][3] = emit_wprep(wo_in, 3, 1.0)
    at["wprep_t"] = cnt["tS"]
    attn_head_late(0)
    attn_head_b(0)
    for j in range(1, NCH):
        attn_kxx_k(j)
        attn_kxx_v(j)
        attn_mixr(j)
        if j + 1 < NCH:
            attn_u2(j + 1)
        attn_head_late(j)
        attn_head_b(j)
        if j >= 2:
            attn_wo_mm(j - 1)
            attn_wo_res(j - 1)
        attn_tail(j)
    attn_wo_mm(NCH - 1)
    attn_wo_res(NCH - 1)

    run("vector", lambda v: v.tensor_copy(sumb[:, 0:8], v3(pbuf, C, 1)))
    v_sum = inc("vector", "vS", lambda v: v.tensor_copy(
        sumb[:, 8:16], v3(qbuf, C, 1)), dep=False)
    wait("scalar", "vS", v_sum)
    d_sum = dma("scalar", DO, cc_in.ap(), sumb[:])
    dwait("gpsimd", d_sum)
    cnt["dC"] += 1
    ops["gpsimd"].append(("o", lambda g: g.collective_compute(
        "AllGather", ALU.bypass,
        replica_groups=[[0, 1, 2, 3], [4, 5, 6, 7]],
        ins=[cc_in.ap()], outs=[cc_out.ap()]).then_inc(sems["dC"])))
    wait("sync", "dC", cnt["dC"])
    d_gs = []
    for r in range(4):
        d_gs.append(dma("sync", DX, gath[:, 16 * r:16 * r + 16],
                        cc_out.ap()[128 * r:128 * r + 128, :]))
    for t in d_gs:
        dwait("gpsimd", t)
    run("gpsimd", lambda g: g.memset(sS(SP8, SP8 + 8), NEG))
    g_f = inc("gpsimd", "gS", lambda g: g.memset(sS(SQ8, SQ8 + 8), 0.0))
    for t in d_gs:
        dwait("vector", t)
    wait("vector", "gS", g_f)
    for jj in range(3):
        run("vector", lambda v, jj=jj: v.tensor_scalar(
            out=sS(FA8, FA8 + 8), in0=gath[:, 16 * jj:16 * jj + 8],
            scalar1=cmask[:, jj:jj + 1], scalar2=cmask[:, 3 + jj:4 + jj],
            op0=ALU.mult, op1=ALU.add))
        run("vector", lambda v, jj=jj: v.tensor_scalar(
            out=sS(FB8, FB8 + 8), in0=gath[:, 16 * jj + 8:16 * jj + 16],
            scalar1=cmask[:, jj:jj + 1], scalar2=0.0,
            op0=ALU.mult, op1=ALU.bypass), dep=False)
        run("vector", lambda v, jj=jj: v.tensor_scalar(
            out=sS(FC8, FC8 + 8), in0=tmix[:, W8:W8 + 8],
            scalar1=cmask[:, 6 + jj:7 + jj], scalar2=0.0,
            op0=ALU.mult, op1=ALU.bypass), dep=False)
        run("vector", lambda v: v.tensor_tensor(
            out=sS(FC8, FC8 + 8), in0=sS(FC8, FC8 + 8), in1=sS(SP8, SP8 + 8),
            op=ALU.add))
        run("vector", lambda v: v.tensor_tensor(
            out=sS(FD8, FD8 + 8), in0=sS(FA8, FA8 + 8), in1=sS(FC8, FC8 + 8),
            op=ALU.max))
        run("vector", lambda v: v.tensor_tensor(
            out=sS(FA8, FA8 + 8), in0=sS(FA8, FA8 + 8), in1=sS(FD8, FD8 + 8),
            op=ALU.subtract))
        v_f = inc("vector", "vS", lambda v: v.tensor_tensor(
            out=sS(FC8, FC8 + 8), in0=sS(FC8, FC8 + 8), in1=sS(FD8, FD8 + 8),
            op=ALU.subtract))
        wait("scalar", "vS", v_f)
        run("scalar", lambda s: s.activation(
            out=sS(FA8, FA8 + 8), in_=sS(FA8, FA8 + 8), func=AF.Exp))
        a_f = inc("scalar", "aS", lambda s: s.activation(
            out=sS(FC8, FC8 + 8), in_=sS(FC8, FC8 + 8), func=AF.Exp))
        wait("vector", "aS", a_f)
        run("vector", lambda v: v.tensor_tensor(
            out=sS(FB8, FB8 + 8), in0=sS(FB8, FB8 + 8), in1=sS(FA8, FA8 + 8),
            op=ALU.mult))
        run("vector", lambda v: v.tensor_tensor(
            out=sS(FC8, FC8 + 8), in0=sS(SQ8, SQ8 + 8), in1=sS(FC8, FC8 + 8),
            op=ALU.mult))
        run("vector", lambda v: v.tensor_tensor(
            out=sS(SQ8, SQ8 + 8), in0=sS(FB8, FB8 + 8), in1=sS(FC8, FC8 + 8),
            op=ALU.add))
        run("vector", lambda v: v.tensor_copy(sS(SP8, SP8 + 8), sS(FD8, FD8 + 8)))
    inc("vector", "vS", lambda v: v.tensor_tensor(
        out=sS(PINW8, PINW8 + 8), in0=sS(SP8, SP8 + 8), in1=tmix[:, W8:W8 + 8],
        op=ALU.add))

    attn_u2(0, redo=True)
    attn_kxx_k(0, redo=True)
    attn_kxx_v(0)
    attn_mixr(0)
    attn_head_late(0)
    attn_head_b(0)
    attn_tail(0)
    attn_wo_mm(0)
    attn_wo_res(0)

    wait("sync", "tS", at["t_k_prev"])
    d_h2 = dma("sync", DX, slot[0][:].rearrange("p (b j) -> p b j", b=DT),
               h_in.rearrange("(b p) j -> p b j", p=128))
    wait("sync", "tS", at["t_r_prev"])
    dwait("sync", st["d_wq_stash"])
    d_wq2 = dma("sync", DX, slot[1][:], wqd)
    dwait("tensor", d_h2)
    dwait("tensor", d_wq2)
    st["last_resid"] = at["last_resid"]
    st["last_t1evac"] = at["u2_prev_evac"]
    st["last_t2evac"] = at["u2_prev_evac"]
    st["half_a"] = [at["a_r"][3], at["a_r"][7]]
    st["half_v"] = [at["pE_wo_free"], 0]
    wait("scalar", "tS", at["t_v_prev"])
    wait("scalar", "gS", max(at["v_xr"], at["v_xv"]))
    xts2 = [x2c[1][:, 0:1024], x2c[1][:, 1024:2048],
            x2c[2][:, 0:1024], x2c[2][:, 1024:2048]]
    units2 = list(range(3, 17)) + [1, 2]
    U2 = st["u"]

    def ffn_load2(k):
        unit = units2[k]
        kw = dict(xt_override=xts2[k % 4])
        if k == 0:
            kw.update(prev_u=U2 - 4, extra_dw=[at["d_xh"]],
                      extra_aS=at["x2c_a"][1])
        elif k == 1:
            kw.update(prev_u=U2 - 2)
        elif k in (2, 3):
            kw.update(prev_u=-1, extra_dw=[at["x3_store"][5]],
                      extra_aS=at["x2c_a"][2])
        if unit in (1, 2):
            kw.setdefault("extra_dw", []).append(at["x3_store"][0])
        ffn_load(unit, 2, **kw)

    ffn_load2(0)
    for k in range(16):
        fu = U2 + k
        ffn_t1(fu)
        if k >= 2:
            ffn_resid(fu - 2)
        if k >= 1:
            ffn_quant(fu - 1)
        ffn_hmm(fu)
        if k >= 1:
            ffn_mm2(fu - 1)
        if k + 1 < 16:
            ffn_load2(k + 1)
        if k >= 2:
            ffn_stat2(fu - 2)
        ffn_stats(fu)
    ffn_resid(U2 + 14)
    ffn_quant(U2 + 15)
    ffn_mm2(U2 + 15)
    ffn_stat2(U2 + 14)
    ffn_resid(U2 + 15)
    ffn_stat2(U2 + 15)

    dwait_all("gpsimd")
    dwait_all("sync")

    def _mk(eng_ops, auto_drain):
        def f(e):
            pending = False
            for kind, fn in eng_ops:
                if kind == "o" and auto_drain and pending:
                    e.drain()
                    pending = False
                fn(e)
                if kind in ("o", "i"):
                    pending = True
        return f

    with nc.Block() as blk:
        blk.sync(_mk(ops["sync"], False))
        blk.tensor(_mk(ops["tensor"], False))
        blk.vector(_mk(ops["vector"], True))
        blk.scalar(_mk(ops["scalar"], True))
        blk.gpsimd(_mk(ops["gpsimd"], True))
    return nc


def _hadamard_bf16():
    h = np.array([[1.0]], dtype=np.float64)
    while h.shape[0] < D:
        h = np.block([[h, h], [h, -h]])
    return (h / np.sqrt(D)).astype(np.float32).astype(ml_dtypes.bfloat16)


def kernel(L0_attn_x, L0_attn_time_mix_k, L0_attn_time_mix_v, L0_attn_time_mix_r,
           L0_attn_W_k, L0_attn_W_v, L0_attn_W_r, L0_attn_W_o,
           L0_attn_time_decay, L0_attn_time_first, L0_ffn_weight):
    x = np.ascontiguousarray(np.asarray(L0_attn_x, np.float32))
    Wk = np.ascontiguousarray(np.asarray(L0_attn_W_k, np.float32))
    Wv = np.ascontiguousarray(np.asarray(L0_attn_W_v, np.float32))
    Wr = np.ascontiguousarray(np.asarray(L0_attn_W_r, np.float32))
    Wo = np.ascontiguousarray(np.asarray(L0_attn_W_o, np.float32))
    Wf = np.ascontiguousarray(np.asarray(L0_ffn_weight, np.float32))
    tmk = np.asarray(L0_attn_time_mix_k, np.float32)
    tmv = np.asarray(L0_attn_time_mix_v, np.float32)
    tmr = np.asarray(L0_attn_time_mix_r, np.float32)
    w = np.asarray(L0_attn_time_decay, np.float32)
    u = np.asarray(L0_attn_time_first, np.float32)

    def fm8(vec):
        return np.ascontiguousarray(vec.reshape(DT, 128).T)

    tmix = np.zeros((128, 40), np.float32)
    tmix[:, 0:8] = fm8(tmk)
    tmix[:, 8:16] = fm8(tmv)
    tmix[:, 16:24] = fm8(tmr)
    tmix[:, 24:32] = fm8(w)
    tmix[:, 32:40] = fm8(u)
    Hb = _hadamard_bf16()
    eye32 = np.eye(128, dtype=np.float32)
    eyebf = np.eye(128, dtype=np.float32).astype(ml_dtypes.bfloat16)
    constv = np.zeros((128, 3), np.float32)
    constv[:, 0] = BIGEPS
    constv[:, 1] = LN_D
    constv[:, 2] = EPS_LN

    nc = build_nc()
    in_maps = []
    for c in range(8):
        b, qd = c // 4, c % 4
        lo = qd * TSH
        shard = np.zeros((TSH + 1, D), np.float32)
        shard[1:] = x[b, lo:lo + TSH]
        if qd > 0:
            shard[0] = x[b, lo - 1]
        cm = np.zeros((128, 9), np.float32)
        for jj in range(3):
            m = 1.0 if jj < qd else 0.0
            cm[:, jj] = m
            cm[:, 3 + jj] = (m - 1.0) * 1e30
            cm[:, 6 + jj] = m * TSH
        in_maps.append({
            "x": shard, "wk": Wk, "wv": Wv, "wr": Wr, "wo": Wo, "wf": Wf,
            "hmat": Hb, "tmix": tmix, "cmask": cm, "eye32": eye32, "eyebf": eyebf,
            "constv": constv,
        })
    res = run_bass_kernel_spmd(nc, in_maps, core_ids=list(range(8)))
    out = np.empty((2, 8192, D), np.float32)
    for c, r in enumerate(res.results):
        out[c // 4, (c % 4) * TSH:(c % 4 + 1) * TSH] = r["out"]
    return out


# version 32
# speedup vs baseline: 1.0617x; 1.0035x over previous
import numpy as np
import ml_dtypes
import concourse.bass as bass
import concourse.mybir as mybir
from concourse.bass_utils import run_bass_kernel_spmd

F32 = mybir.dt.float32
BF16 = mybir.dt.bfloat16
AF = mybir.ActivationFunctionType
ALU = mybir.AluOpType
AX = mybir.AxisListType

D = 1024
DT = 8
TSH = 2048
C = 256
NCH = TSH // C
MAGIC = 12582912.0
NEG = -1e30
EPS_LN = 1e-5
BIGEPS = float(np.float32(D * D * EPS_LN))
LN_D = float(np.float32(np.log(np.float32(D))))

ENGS = ("sync", "tensor", "vector", "scalar", "gpsimd")


def build_nc():
    nc = bass.Bass("TRN2", target_bir_lowering=False)

    x_in = nc.dram_tensor("x", [TSH + 1, D], F32, kind="ExternalInput").ap()
    wk_in = nc.dram_tensor("wk", [D, D], F32, kind="ExternalInput").ap()
    wv_in = nc.dram_tensor("wv", [D, D], F32, kind="ExternalInput").ap()
    wr_in = nc.dram_tensor("wr", [D, D], F32, kind="ExternalInput").ap()
    wo_in = nc.dram_tensor("wo", [D, D], F32, kind="ExternalInput").ap()
    wf_in = nc.dram_tensor("wf", [D, D], F32, kind="ExternalInput").ap()
    h_in = nc.dram_tensor("hmat", [D, D], BF16, kind="ExternalInput").ap()
    tmix_in = nc.dram_tensor("tmix", [128, 40], F32, kind="ExternalInput").ap()
    cmask_in = nc.dram_tensor("cmask", [128, 9], F32, kind="ExternalInput").ap()
    eye32_in = nc.dram_tensor("eye32", [128, 128], F32, kind="ExternalInput").ap()
    eyebf_in = nc.dram_tensor("eyebf", [128, 128], BF16, kind="ExternalInput").ap()
    constv_in = nc.dram_tensor("constv", [128, 3], F32, kind="ExternalInput").ap()
    out_d = nc.dram_tensor("out", [TSH, D], F32, kind="ExternalOutput").ap()

    x2d = nc.dram_tensor("x2d", [TSH + 1, D], F32, kind="Internal").ap()
    x3d = nc.dram_tensor("x3d", [TSH, D], F32, kind="Internal").ap()
    wqd = nc.dram_tensor("wqd", [128, DT * D], BF16, kind="Internal").ap()
    cc_in = nc.dram_tensor("cc_in", [128, 16], F32, kind="Internal")
    cc_out = nc.dram_tensor("cc_out", [512, 16], F32, kind="Internal")

    slot = [nc.alloc_sbuf_tensor(f"wslot{i}", [128, DT * D], BF16) for i in range(4)]
    u2s = [nc.alloc_sbuf_tensor(f"u2s{i}", [128, DT * (C + 1)], F32) for i in range(2)]
    pbuf = nc.alloc_sbuf_tensor("pbuf", [128, DT * (C + 1)], F32)
    qbuf = nc.alloc_sbuf_tensor("qbuf", [128, DT * (C + 1)], F32)
    kbuf = nc.alloc_sbuf_tensor("kbuf", [128, DT * C], F32)
    abuf = nc.alloc_sbuf_tensor("abuf", [128, DT * C], F32)
    vbuf = nc.alloc_sbuf_tensor("vbuf", [128, DT * C], F32)
    rtbuf = nc.alloc_sbuf_tensor("rtbuf", [128, DT * C], F32)
    sc1 = nc.alloc_sbuf_tensor("sc1", [128, DT * C], F32)
    sc2 = nc.alloc_sbuf_tensor("sc2", [128, DT * C], F32)
    sc3 = nc.alloc_sbuf_tensor("sc3", [128, DT * C], F32)
    sc4 = nc.alloc_sbuf_tensor("sc4", [128, DT * C], F32)
    mixb = nc.alloc_sbuf_tensor("mixb", [128, DT * C], BF16)
    mixb2 = nc.alloc_sbuf_tensor("mixb2", [128, DT * C], BF16)
    mixb3 = nc.alloc_sbuf_tensor("mixb3", [128, DT * C], BF16)
    ybuf = nc.alloc_sbuf_tensor("ybuf", [128, DT * C], BF16)
    x2c = [nc.alloc_sbuf_tensor(f"x2c{i}", [128, 2 * D], F32) for i in range(3)]
    u2tm = nc.alloc_sbuf_tensor("u2tm", [128, D], F32)
    tmix = nc.alloc_sbuf_tensor("tmix_s", [128, 40], F32)
    tmixg = nc.alloc_sbuf_tensor("tmixg_s", [128, 40], F32)
    cmask = nc.alloc_sbuf_tensor("cmask_s", [128, 9], F32)
    eye32 = nc.alloc_sbuf_tensor("eye32_s", [128, 128], F32)
    eyebf = nc.alloc_sbuf_tensor("eyebf_s", [128, 128], BF16)
    iota = nc.alloc_sbuf_tensor("iota_s", [128, C], F32)
    rstd2 = nc.alloc_sbuf_tensor("rstd2", [128, 17], F32)
    nmr2 = nc.alloc_sbuf_tensor("nmr2", [128, 17], F32)
    smallS = nc.alloc_sbuf_tensor("smallS", [128, 128], F32)
    bnb = nc.alloc_sbuf_tensor("bnb", [128, 12], F32)
    constv = nc.alloc_sbuf_tensor("constv_s", [128, 3], F32)
    sumb = nc.alloc_sbuf_tensor("sumb", [128, 16], F32)
    gath = nc.alloc_sbuf_tensor("gath", [128, 64], F32)

    pE = nc.alloc_psum_tensor("pE", [128, 2048], F32)
    pA = nc.alloc_psum_tensor("pA", [128, 1024], F32)
    pB = nc.alloc_psum_tensor("pB", [128, 1024], F32)
    pBbf = pB[:].bitcast(BF16)
    pAbf = pA[:].bitcast(BF16)
    nc.const_aps.register(F32, BIGEPS, constv[:, 0:1])
    nc.const_aps.register(F32, LN_D, constv[:, 1:2])
    nc.const_aps.register(F32, EPS_LN, constv[:, 2:3])

    sems, cnt, waited = {}, {}, {}
    DX = ["dXa", "dXb", "dXc", "dXd"]
    DO = ["dOa", "dOb", "dOc", "dOd"]
    for s in DX + DO + ["dC", "tS", "vS", "aS", "gS"]:
        sems[s] = nc.alloc_semaphore(s)
        cnt[s] = 0
    pool_idx = {"dX": 0, "dO": 0}
    ops = {e: [] for e in ENGS}

    def wait(eng, sname, val):
        if val <= 0:
            return
        assert val <= cnt[sname], f"wait ahead of inc: {eng} {sname} {val}>{cnt[sname]}"
        key = (eng, sname)
        if waited.get(key, 0) >= val:
            return
        waited[key] = val
        ops[eng].append(("w", lambda e, s=sems[sname], v=val: e.wait_ge(s, v)))

    def run(eng, fn, dep=True):
        ops[eng].append(("o" if dep else "i", fn))

    def inc(eng, sname, fn, dep=True):
        cnt[sname] += 1
        v = cnt[sname]
        ops[eng].append(("o" if dep else "i",
                         lambda e, f=fn, s=sems[sname]: f(e).then_inc(s, 1)))
        return v

    def dma(eng, pool, out_ap, in_ap):
        pname = "dX" if pool is DX else "dO"
        sname = pool[pool_idx[pname] % len(pool)]
        pool_idx[pname] += 1
        wait(eng, sname, cnt[sname])
        cnt[sname] += 16
        v = cnt[sname]
        ops[eng].append(
            ("d", lambda e, o=out_ap, i=in_ap, s=sems[sname]:
             e.dma_start(o, i).then_inc(s, 16)))
        return (sname, v)

    def dwait(eng, tok):
        if tok:
            wait(eng, tok[0], tok[1])

    def dwait_all(eng):
        for s in DX + DO:
            wait(eng, s, cnt[s])

    def barrier(engs=ENGS, sems_list=("tS", "vS", "aS", "gS")):
        for e in engs:
            for s in sems_list:
                wait(e, s, cnt[s])
            dwait_all(e)

    def v3(t, off, n):
        return t[:].rearrange("p (b c) -> p b c", b=DT)[:, :, off:off + n]

    def r8(t):
        return t[:].rearrange("p (b c) -> p b c", b=DT)

    def bc3(col_ap):
        return col_ap.unsqueeze(2).broadcast_to([128, DT, C])

    SP8, SQ8, FA8, FB8, FC8, FD8, PINW8, EW8 = 64, 72, 80, 88, 96, 104, 112, 120
    W8 = 24
    U8 = 32

    def sS(a, b=None):
        return smallS[:, a:a + 1] if b is None else smallS[:, a:b]

    d_cv = dma("sync", DX, constv[:], constv_in)
    d_tm = dma("sync", DX, tmix[:], tmix_in)
    d_tg = dma("sync", DX, tmixg[:], tmix_in)
    d_cm = dma("sync", DX, cmask[:], cmask_in)
    d_e3 = dma("sync", DX, eye32[:], eye32_in)
    d_eb = dma("sync", DX, eyebf[:], eyebf_in)
    d_h1 = dma("sync", DX, slot[0][:].rearrange("p (b j) -> p b j", b=DT),
               h_in.rearrange("(b p) j -> p b j", p=128))
    const_toks = [d_cv, d_tm, d_tg, d_cm, d_e3, d_eb, d_h1]

    def wait_consts(eng):
        for t in const_toks:
            dwait(eng, t)

    g_iota = inc("gpsimd", "gS", lambda g: g.iota(
        iota[:], pattern=[[1, C]], base=0, channel_multiplier=0,
        allow_small_or_imprecise_dtypes=True))
    dwait("scalar", d_tm)
    a_ew = inc("scalar", "aS", lambda s: s.activation(
        out=sS(EW8, EW8 + 8), in_=tmix[:, W8:W8 + 8], func=AF.Exp))

    def emit_wqprep():
        stg = [kbuf[:, 0:1024], kbuf[:, 1024:2048]]
        vq_by_par = [0, 0]
        for et in range(DT):
            par = et % 2
            wait("sync", "vS", vq_by_par[par])
            d_l = dma("sync", DX, stg[par], wf_in[128 * et:128 * et + 128, :])
            dwait("vector", d_l)
            s = stg[par]
            run("vector", lambda v, s=s: v.tensor_reduce(
                out=sS(58), in_=s, axis=AX.X, op=ALU.add, apply_absolute_value=True))
            run("vector", lambda v: v.tensor_scalar(
                out=sS(59), in0=sS(58), scalar1=1.0 / D, scalar2=EPS_LN,
                op0=ALU.mult, op1=ALU.max))
            run("vector", lambda v: v.tensor_scalar(
                out=sS(60), in0=sS(59), scalar1=0.5, scalar2=0.0,
                op0=ALU.mult, op1=ALU.bypass))
            run("vector", lambda v: v.tensor_scalar(
                out=sS(61), in0=sS(59), scalar1=-0.5, scalar2=0.0,
                op0=ALU.mult, op1=ALU.bypass), dep=False)
            run("vector", lambda v, s=s: v.tensor_scalar(
                out=abuf[:, 0:1024], in0=s, scalar1=sS(60), scalar2=0.0,
                op0=ALU.is_gt, op1=ALU.bypass))
            run("vector", lambda v, s=s: v.tensor_scalar(
                out=abuf[:, 1024:2048], in0=s, scalar1=sS(61), scalar2=0.0,
                op0=ALU.is_lt, op1=ALU.bypass), dep=False)
            run("vector", lambda v: v.tensor_tensor(
                out=vbuf[:, 0:1024], in0=abuf[:, 0:1024], in1=abuf[:, 1024:2048],
                op=ALU.subtract))
            vq_by_par[par] = inc("vector", "vS", lambda v, et=et: v.tensor_scalar(
                out=slot[1][:, et * 1024:(et + 1) * 1024], in0=vbuf[:, 0:1024],
                scalar1=sS(59), scalar2=0.0, op0=ALU.mult, op1=ALU.bypass))
        st["vq_all"] = cnt["vS"]
        wait("tensor", "vS", st["vq_all"])
        wait_consts("tensor")
        aq_half = [0, 0]
        aq = 0
        for et in range(DT):
            hb = et % 2
            if aq_half[hb]:
                wait("tensor", "aS", aq_half[hb])
            tq = 0
            for ib in range(DT):
                tq = inc("tensor", "tS", lambda t, et=et, ib=ib, hb=hb: t.transpose(
                    pBbf[:, hb * 1024 + ib * 128: hb * 1024 + (ib + 1) * 128],
                    in_=slot[1][:, et * 1024 + ib * 128: et * 1024 + ib * 128 + 128],
                    identity=eyebf[:]), dep=False)
            wait("scalar", "tS", tq)
            aq = inc("scalar", "aS", lambda s, et=et, hb=hb: s.activation(
                out=slot[2][:].rearrange("p (b c) -> p b c", b=DT)[
                    :, :, et * 128:(et + 1) * 128],
                in_=pBbf[:, hb * 1024:(hb + 1) * 1024].rearrange(
                    "p (b c) -> p b c", b=DT),
                func=AF.Copy))
            aq_half[hb] = aq
        wait("scalar", "aS", aq)
        st["d_wq_stash"] = dma("scalar", DO, wqd, slot[2][:])

    st = {"last_resid": 0, "last_t1evac": 0, "last_t2evac": 0,
          "last_store": {}, "u": 0, "last_t2_t": 0,
          "half_a": [0, 0], "half_v": [0, 0], "fstash": {}, "s2stash": {},
          "stat_free_a": [0, 0, 0, 0]}

    ffn_xts = [x2c[0][:, 0:1024], x2c[1][:, 0:1024],
               x2c[0][:, 1024:2048], x2c[1][:, 1024:2048]]

    def ffn_load(unit, mode, xt_override=None, prev_u=None, extra_dw=(),
                 extra_aS=0):
        u = st["u"]
        st["u"] += 1
        pu = u % 2
        pu4 = u % 4
        xt = xt_override if xt_override is not None else ffn_xts[pu4]
        xfm = mixb2[:, pu * 1024:(pu + 1) * 1024]
        base = pu4 * 16
        cM, cV, cR1, cM32 = base, base + 1, base + 2, base + 3
        if mode == 1:
            src_ap = x_in[0:128, :] if unit == 0 else \
                x_in[1 + 128 * (unit - 1): 129 + 128 * (unit - 1), :]
        else:
            src_ap = x3d[128 * (unit - 1): 128 * (unit - 1) + 128, :]

        prev = st["last_store"].get(u - 4 if prev_u is None else prev_u)
        if prev is not None:
            dwait("sync", prev[0])
            if prev[1]:
                wait("sync", "vS", prev[1])
        for tok in extra_dw:
            dwait("sync", tok)
        if extra_aS:
            wait("sync", "aS", extra_aS)
        d_x = dma("sync", DX, xt, src_ap)
        st["fstash"][u] = dict(unit=unit, mode=mode, pu=pu, pu4=pu4, base=base,
                               hh=u % 2, xt=xt, xfm=xfm, d_x=d_x,
                               cM=cM, cV=cV, cR1=cR1, cM32=cM32)

    def ffn_stats(u):
        fs = st["fstash"][u]
        pu4, xt = fs["pu4"], fs["xt"]
        cM, cV, cR1, cM32 = fs["cM"], fs["cV"], fs["cR1"], fs["cM32"]
        dwait("vector", fs["d_x"])
        wait("vector", "aS", st["stat_free_a"][pu4])
        run("vector", lambda v, xt=xt: v.bn_stats(out=bnb[:, 0:6], in_=xt[:, 0:512]))
        run("vector", lambda v, xt=xt: v.bn_stats(out=bnb[:, 6:12],
                                                  in_=xt[:, 512:1024]), dep=False)
        v_agg = inc("vector", "vS", lambda v: v.bn_aggr(
            out=sS(cM, cM + 2), in_=bnb[:]))
        wait("scalar", "vS", v_agg)
        run("scalar", lambda s: s.activation(
            out=sS(cR1), in_=sS(cV), func=AF.Ln, bias=EPS_LN))
        a_rstd1 = inc("scalar", "aS", lambda s: s.activation(
            out=sS(cR1), in_=sS(cR1), func=AF.Exp, scale=-0.5))
        a_m32 = inc("scalar", "aS", lambda s: s.activation(
            out=sS(cM32), in_=sS(cM), func=AF.Identity, scale=32.0), dep=False)
        fs["a_rstd1"], fs["a_m32"] = a_rstd1, a_m32

    def ffn_t1(u):
        fs = st["fstash"][u]
        xt, xfm = fs["xt"], fs["xfm"]
        dwait("tensor", fs["d_x"])
        wait("tensor", "aS", max(st["last_t1evac"], st["last_t2evac"]))
        t_t1 = 0
        for b in range(DT):
            t_t1 = inc("tensor", "tS", lambda t, xt=xt, b=b: t.transpose(
                pA[:, b * 128:(b + 1) * 128], in_=xt[:, b * 128:(b + 1) * 128],
                identity=eye32[:]), dep=False)
        wait("scalar", "tS", t_t1)
        a_t1 = inc("scalar", "aS", lambda s, xfm=xfm: s.activation(
            out=xfm, in_=pA[:], func=AF.Copy))
        st["last_t1evac"] = a_t1
        fs["a_t1"] = a_t1

    def ffn_hmm(u):
        fs = st["fstash"][u]
        hh, xfm = fs["hh"], fs["xfm"]
        wait("tensor", "aS", max(fs["a_t1"], st["half_a"][hh]))
        wait("tensor", "vS", st["half_v"][hh])
        for jc in range(2):
            for db in range(DT):
                inc("tensor", "tS", lambda t, xfm=xfm, jc=jc, db=db, hh=hh: t.matmul(
                    pE[:, hh * 1024 + jc * 512: hh * 1024 + (jc + 1) * 512],
                    lhsT=xfm[:, db * 128:(db + 1) * 128],
                    rhs=slot[0][:, db * 1024 + jc * 512: db * 1024 + jc * 512 + 512],
                    start=(db == 0), stop=(db == DT - 1)), dep=False)
        fs["t_h"] = cnt["tS"]

    def ffn_quant(u):
        fs = st["fstash"][u]
        pu, hh = fs["pu"], fs["hh"]
        base = fs["base"]
        cM32, cR1 = fs["cM32"], fs["cR1"]
        cMX, cT, cRT, cCC, cCS, cAB = (base + 4, base + 5, base + 6, base + 7,
                                       base + 8, base + 9)
        zb = sc1[:, pu * 1024:(pu + 1) * 1024]
        q1 = sc2[:, pu * 1024:(pu + 1) * 1024]
        Qb = mixb[:, 0:1024]
        hE = pE[:, hh * 1024:(hh + 1) * 1024]

        wait("vector", "tS", max(fs["t_h"], st["last_t2_t"]))
        wait("vector", "aS", fs["a_m32"])
        run("vector", lambda v, hE=hE: v.tensor_reduce(
            out=sS(cMX), in_=hE[:, 1:1024], axis=AX.X, op=ALU.max,
            apply_absolute_value=True), dep=False)
        run("vector", lambda v, hE=hE: v.tensor_scalar(
            out=hE[:, 0:1], in0=hE[:, 0:1], scalar1=sS(cM32), scalar2=0.0,
            op0=ALU.subtract, op1=ALU.bypass), dep=False)
        run("vector", lambda v, hE=hE: v.tensor_scalar(
            out=sS(cAB), in0=hE[:, 0:1], scalar1=-1.0, scalar2=0.0,
            op0=ALU.mult, op1=ALU.max))
        run("vector", lambda v: v.tensor_tensor(
            out=sS(cMX), in0=sS(cMX), in1=sS(cAB), op=ALU.max))
        wait("vector", "aS", fs["a_rstd1"])
        run("vector", lambda v: v.tensor_scalar(
            out=sS(cT), in0=sS(cMX), scalar1=sS(cR1), scalar2=EPS_LN,
            op0=ALU.mult, op1=ALU.max))
        run("vector", lambda v: v.reciprocal(out=sS(cRT), in_=sS(cT)))
        run("vector", lambda v: v.tensor_scalar(
            out=sS(cCC), in0=sS(cRT), scalar1=sS(cR1), scalar2=7.0,
            op0=ALU.mult, op1=ALU.mult))
        v_coef = inc("vector", "vS", lambda v: v.tensor_scalar(
            out=sS(cCS), in0=sS(cT), scalar1=0.5 / 7.0, scalar2=0.0,
            op0=ALU.mult, op1=ALU.bypass), dep=False)
        wait("scalar", "vS", v_coef)
        a_z = inc("scalar", "aS", lambda s, zb=zb, hE=hE: s.activation(
            out=zb, in_=hE, func=AF.Copy, scale=sS(cCC), bias=MAGIC))
        st["half_a"][hh] = a_z
        st["half_v"][hh] = v_coef
        wait("vector", "aS", a_z)
        run("vector", lambda v, zb=zb, q1=q1: v.tensor_scalar(
            out=q1, in0=zb, scalar1=MAGIC, scalar2=-8.0,
            op0=ALU.subtract, op1=ALU.max), dep=False)
        v_q = inc("vector", "vS", lambda v, q1=q1: v.tensor_scalar(
            out=Qb, in0=q1, scalar1=7.0, scalar2=sS(cCS),
            op0=ALU.min, op1=ALU.mult))
        pu4 = fs["pu4"]
        st["stat_free_a"][pu4] = max(st["stat_free_a"][pu4], fs["a_m32"])
        fs["v_q"], fs["a_z"] = v_q, a_z

    def ffn_mm2(u):
        fs = st["fstash"][u]
        unit, mode, pu, hh, xt = fs["unit"], fs["mode"], fs["pu"], fs["hh"], fs["xt"]
        cM, cV = fs["cM"], fs["cV"]
        Qb = mixb[:, 0:1024]
        Qfm = mixb[:, 1024:2048]
        if mode == 1:
            dst = x2d[0:128, :] if unit == 0 else \
                x2d[1 + 128 * (unit - 1): 129 + 128 * (unit - 1), :]
            stat_col = unit
            wslot_q = 2
        else:
            dst = out_d[128 * (unit - 1): 128 * (unit - 1) + 128, :]
            stat_col = None
            wslot_q = 1
        wait("tensor", "vS", fs["v_q"])
        wait("tensor", "aS", st["last_t1evac"])
        t_t2 = 0
        for b in range(DT):
            t_t2 = inc("tensor", "tS", lambda t, b=b: t.transpose(
                pAbf[:, b * 128:(b + 1) * 128],
                in_=Qb[:, b * 128:(b + 1) * 128], identity=eyebf[:]), dep=False)
        st["last_t2_t"] = t_t2
        wait("scalar", "tS", t_t2)
        a_t2 = inc("scalar", "aS", lambda s: s.activation(
            out=Qfm, in_=pAbf[:, 0:1024], func=AF.Copy))
        st["last_t2evac"] = a_t2
        wait("tensor", "aS", a_t2)
        wait("tensor", "vS", st["last_resid"])
        t_w = 0
        for oc in range(2):
            for ib in range(DT):
                t_w = inc("tensor", "tS", lambda t, oc=oc, ib=ib, ws=wslot_q: t.matmul(
                    pB[:, oc * 512:(oc + 1) * 512],
                    lhsT=Qfm[:, ib * 128:(ib + 1) * 128],
                    rhs=slot[ws][:, ib * 1024 + oc * 512: ib * 1024 + oc * 512 + 512],
                    start=(ib == 0), stop=(ib == DT - 1)), dep=False)
        fs["t_w"] = t_w
        fs["dst"] = dst

    def ffn_resid(u):
        fs = st["fstash"].pop(u)
        unit, mode, xt, dst = fs["unit"], fs["mode"], fs["xt"], fs["dst"]
        wait("vector", "tS", fs["t_w"])
        v_res = inc("vector", "vS", lambda v, xt=xt: v.tensor_tensor(
            out=xt, in0=xt, in1=pB[:], op=ALU.add), dep=False)
        st["last_resid"] = v_res
        wait("scalar", "vS", v_res)
        if mode == 1 and unit == 0:
            d_s = dma("scalar", DO, x2d[0:1, :], xt[0:1, :])
        else:
            d_s = dma("scalar", DO, dst, xt)
        st["last_store"][u] = (d_s, 0)
        fs["d_s"] = d_s
        st["s2stash"][u] = fs

    def ffn_stat2(u):
        fs = st["s2stash"].pop(u)
        if fs["mode"] != 1:
            return
        pu4, xt, unit = fs["pu4"], fs["xt"], fs["unit"]
        cM, cV = fs["cM"], fs["cV"]
        stat_col = unit
        run("vector", lambda v, xt=xt: v.bn_stats(out=bnb[:, 0:6],
                                                  in_=xt[:, 0:512]))
        run("vector", lambda v, xt=xt: v.bn_stats(out=bnb[:, 6:12],
                                                  in_=xt[:, 512:1024]), dep=False)
        v_agg2 = inc("vector", "vS", lambda v: v.bn_aggr(
            out=sS(cM, cM + 2), in_=bnb[:]))
        wait("scalar", "vS", v_agg2)
        run("scalar", lambda s, sc=stat_col: s.activation(
            out=rstd2[:, sc:sc + 1], in_=sS(cV), func=AF.Ln, bias=EPS_LN))
        a_r2 = inc("scalar", "aS", lambda s, sc=stat_col: s.activation(
            out=rstd2[:, sc:sc + 1], in_=rstd2[:, sc:sc + 1],
            func=AF.Exp, scale=-0.5))
        wait("vector", "aS", a_r2)
        v_stat2 = inc("vector", "vS", lambda v, sc=stat_col: v.tensor_scalar(
            out=nmr2[:, sc:sc + 1], in0=sS(cM),
            scalar1=rstd2[:, sc:sc + 1], scalar2=-1.0,
            op0=ALU.mult, op1=ALU.mult), dep=False)
        st["stat_free_a"][pu4] = max(st["stat_free_a"][pu4], a_r2)
        st["last_store"][u] = (fs["d_s"], v_stat2)

    at = {"u2_prev_evac": 0, "x3_store": {}, "last_revac_et": [0] * DT,
          "last_resid": 0, "u2_free_g": {0: 0, 1: 0}, "u2_free_v": {0: 0, 1: 0},
          "last_u2t": 0, "x2c_a": {0: 0, 1: 0, 2: 0}, "mslot": 0,
          "tail_v": {}, "u2_ev": {}, "wo_stash": {}, "wores": {},
          "w_ready": {},
          "abuf_free_v": 0}


    def emit_proj(ws, wait_g, prev_evacs, mb):
        tks = []
        wait("tensor", "aS", at["w_ready"].get(ws, 0))
        wait("tensor", "gS", wait_g)
        for et in range(DT):
            wait("tensor", "aS", prev_evacs[et])
            tm = 0
            for db in range(DT):
                tm = inc("tensor", "tS", lambda t, et=et, db=db, ws=ws, mb=mb:
                         t.matmul(
                    pE[:, et * C:(et + 1) * C],
                    lhsT=slot[ws][:, db * 1024 + et * 128: db * 1024 + et * 128 + 128],
                    rhs=mb[:, db * C:(db + 1) * C],
                    start=(db == 0), stop=(db == DT - 1)), dep=False)
            tks.append(tm)
        return tks

    def emit_evacs(tks, out_t, func, scale=1.0):
        evs = []
        for ep in range(DT // 2):
            wait("scalar", "tS", tks[2 * ep + 1])
            e = inc("scalar", "aS", lambda s, ep=ep, o=out_t, f=func,
                    sc=scale: s.activation(
                out=o[:, ep * 2 * C:(ep + 1) * 2 * C],
                in_=pE[:, ep * 2 * C:(ep + 1) * 2 * C],
                func=f, scale=sc), dep=False)
            evs.extend([e, e])
        return evs

    def emit_mix8(base_col, mb, u2c, wait_t):
        if wait_t:
            wait("gpsimd", "tS", wait_t)
        run("gpsimd", lambda g, bc=base_col, mb=mb: g.tensor_tensor(
            out=r8(mb), in0=r8(sc1), in1=bc3(tmixg[:, bc:bc + 8]), op=ALU.mult))
        gm = inc("gpsimd", "gS", lambda g, mb=mb, u2c=u2c: g.tensor_tensor(
            out=r8(mb), in0=r8(mb), in1=v3(u2c, 0, C), op=ALU.add))
        return gm

    def attn_u2(j, redo=False):
        pu = 0 if redo else j % 2
        bu = 0 if redo else j % 3
        u2c = u2s[pu]
        u2p = u2s[1 - pu]
        first = (j == 0)

        def dw_store(u, full=False):
            p = st["last_store"].get(u)
            if p is not None:
                dwait("sync", p[0])
                if full and p[1]:
                    wait("sync", "vS", p[1])

        if redo:
            dwait("sync", at["x3_store"][NCH - 2])
            dwait("sync", at["x3_store"][NCH - 1])
        else:
            xs = at["x3_store"].get(j - 3)
            if xs is not None:
                dwait("sync", xs)
            dw_store(2 * j + 1)
            dw_store(2 * j + 2)
            if j == 0:
                dw_store(14, full=True)
                dw_store(16, full=True)
                dw_store(13, full=True)
                dw_store(0)
            if j == 1:
                dw_store(13, full=True)
                dw_store(15, full=True)
        wait("sync", "aS", at["x2c_a"][bu])
        if first:
            wait("sync", "aS", at["x2c_a"][1])
        d_x2 = dma(
            "sync", DX,
            x2c[bu][:].rearrange("p (a d) -> p a d", a=2),
            x2d[1 + C * j: 1 + C * (j + 1), :].rearrange("(a p) d -> p a d", p=128))
        d_xh = None
        if first:
            d_xh = dma("sync", DX, x2c[1][:, 0:1024], x2d[0:128, :])
            at["d_xh"] = d_xh

        def u2_build(src_ap, stat_col, off, d_dep, width=128):
            dwait("scalar", d_dep)
            wait("scalar", "gS", at["u2_free_g"][pu])
            wait("scalar", "vS", at["u2_free_v"][pu])
            wait("scalar", "tS", at["last_u2t"])
            a_ap = inc("scalar", "aS", lambda s, src=src_ap, sc=stat_col: s.activation(
                out=u2tm[:], in_=src, func=AF.Identity,
                scale=rstd2[:, sc:sc + 1], bias=nmr2[:, sc:sc + 1]))
            wait("tensor", "aS", max(a_ap, at["u2_prev_evac"]))
            tt = 0
            for b in range(DT):
                tt = inc("tensor", "tS", lambda t, b=b: t.transpose(
                    pA[:, b * 128:(b + 1) * 128], in_=u2tm[:, b * 128:(b + 1) * 128],
                    identity=eye32[:]), dep=False)
            at["last_u2t"] = tt
            wait("scalar", "tS", tt)
            ev = inc("scalar", "aS", lambda s, off=off, w=width, u2c=u2c: s.activation(
                out=v3(u2c, off, w),
                in_=pA[:].rearrange("p (b c) -> p b c", b=DT)[:, :, 0:w],
                func=AF.Copy))
            at["u2_prev_evac"] = ev
            return ev

        if first:
            ev = u2_build(x2c[1][:, 0:1024], 0, 0, d_xh, width=1)
            at["x2c_a"][1] = ev
            ev = u2_build(x2c[bu][:, 0:1024], 1, 1, d_x2)
            ev = u2_build(x2c[bu][:, 1024:2048], 2, 129, d_x2)
            at["x2c_a"][bu] = ev
        else:
            wait("gpsimd", "aS", at["u2_prev_evac"])
            wait("gpsimd", "vS", max(at["u2_free_v"][1 - pu],
                                     at["u2_free_v"][pu]))
            wait("gpsimd", "gS", at["u2_free_g"][pu])
            g_h = inc("gpsimd", "gS", lambda g, u2c=u2c, u2p=u2p: g.tensor_copy(
                v3(u2c, 0, 1), v3(u2p, C, 1)))
            at["u2_free_g"][1 - pu] = g_h
            at["halo_g"] = g_h
            ev = u2_build(x2c[bu][:, 0:1024], 2 * j + 1, 1, d_x2)
            ev = u2_build(x2c[bu][:, 1024:2048], 2 * j + 2, 129, d_x2)
            at["x2c_a"][bu] = ev
        u2cv = u2c[:].rearrange("p (b c) -> p b c", b=DT)
        wait_consts("gpsimd")
        wait("gpsimd", "aS", ev)
        wait("gpsimd", "vS", at.get("sc1_dve_free", 0))
        inc("gpsimd", "gS", lambda g, u2cv=u2cv: g.tensor_tensor(
            out=r8(sc1), in0=u2cv[:, :, 1:1 + C], in1=u2cv[:, :, 0:C],
            op=ALU.subtract))
        v_xk = emit_mix8(0, mixb, u2c,
                         max(at.get("t_k_prev", 0), at.get("mix_pe_free", 0)))
        v_xv = emit_mix8(8, mixb2, u2c,
                         max(at.get("t_v_prev", 0), at.get("mix_pe_free", 0)))
        at["u2_ev"][j] = (ev, pu, v_xk, v_xv)

    def attn_kxx_k(j, redo=False):
        ev, pu, v_xk, v_xv = at["u2_ev"].pop(j)
        u2c = u2s[pu]
        at["u2c"] = u2c
        at["pu"] = pu
        at["v_xv"] = v_xv
        wait("tensor", "vS", at.get("pE_wo_free", 0))
        t_k = emit_proj(0, v_xk, at["last_revac_et"], mixb)
        at["t_k_prev"] = t_k[DT - 1]
        wait("scalar", "gS", at.get("g_a", 0))
        a_k = emit_evacs(t_k, kbuf, AF.Copy)
        at["t_k"], at["a_k"] = t_k, a_k
        at["first"] = (j == 0)
        at["redo"] = redo
        at["j"] = j

    def attn_kxx_v(j):
        a_k = at["a_k"]
        v_xv = at["v_xv"]
        wait("gpsimd", "vS", max(at.get("tail_vg", 0), at.get("v_qs", 0)))
        if j == 0:
            g_pb = inc("gpsimd", "gS", lambda g: g.memset(v3(pbuf, 0, 1), NEG))
            wait("gpsimd", "vS", at.get("tail_vn1", 0))
            g_qb = inc("gpsimd", "gS", lambda g: g.memset(v3(qbuf, 0, 1), 0.0))
        else:
            g_pb = inc("gpsimd", "gS", lambda g: g.tensor_copy(
                v3(pbuf, 0, 1), v3(pbuf, C, 1)))
            wait("gpsimd", "vS", at.get("tail_vn1", 0))
            g_qb = inc("gpsimd", "gS", lambda g: g.tensor_copy(
                v3(qbuf, 0, 1), v3(qbuf, C, 1)))
        at["g_pb"], at["g_qb"] = g_pb, g_qb
        if not (j == 0 and not at["redo"]):
            wait("gpsimd", "tS", at.get("wprep_t", 0))
            wait("gpsimd", "aS", a_k[DT - 1])
            wait("gpsimd", "vS", at["abuf_free_v"])
            g_a = inc("gpsimd", "gS", lambda g: g.tensor_tensor(
                out=r8(abuf), in0=r8(kbuf), in1=bc3(tmixg[:, U8:U8 + 8]),
                op=ALU.add))
            at["g_a"] = g_a
        t_v = emit_proj(2, v_xv, a_k, mixb2)
        at["t_v_prev"] = t_v[DT - 1]
        wait("scalar", "vS", at.get("vbuf_free", 0))
        a_v = emit_evacs(t_v, vbuf, AF.Copy)
        at["t_v"], at["a_v"] = t_v, a_v

    def attn_mixr(j):
        at["v_xr"] = emit_mix8(16, mixb3, at["u2c"], at.get("t_r_prev", 0))

    def attn_head_late(j):
        first, redo = at["first"], at["redo"]
        t_k, a_k = at["t_k"], at["a_k"]
        pu = at["pu"]
        u2c = at["u2c"]

        wait("vector", "gS", at["g_pb"])
        for b in range(DT):
            wait("vector", "aS", a_k[b])
            inc("vector", "vS", lambda v, b=b: v.tensor_tensor_scan(
                out=pbuf[:, b * (C + 1) + 1: b * (C + 1) + 1 + C],
                data0=tmix[:, W8 + b:W8 + b + 1].broadcast_to([128, C]),
                data1=kbuf[:, b * C:(b + 1) * C],
                initial=pbuf[:, b * (C + 1): b * (C + 1) + 1],
                op0=ALU.add, op1=ALU.max), dep=(b == 0))
        t_v, a_v = at["t_v"], at["a_v"]
        run("vector", lambda v: v.tensor_tensor(
            out=r8(sc2), in0=v3(pbuf, 0, C), in1=bc3(tmix[:, W8:W8 + 8]),
            op=ALU.add))
        v_al0 = inc("vector", "vS", lambda v: v.tensor_tensor(
            out=r8(sc3), in0=r8(sc2), in1=v3(pbuf, 1, C), op=ALU.subtract))
        wait("scalar", "vS", v_al0)
        a_ea = inc("scalar", "aS", lambda s: s.activation(
            out=sc3[:], in_=sc3[:], func=AF.Exp))
        v_be = inc("vector", "vS", lambda v: v.tensor_tensor(
            out=r8(sc2), in0=r8(kbuf), in1=v3(pbuf, 1, C), op=ALU.subtract))
        wait("scalar", "vS", v_be)
        a_be = inc("scalar", "aS", lambda s: s.activation(
            out=sc2[:], in_=sc2[:], func=AF.Exp))
        wait("vector", "aS", max(a_be, a_v[DT - 1]))
        v_bv = inc("vector", "vS", lambda v: v.tensor_tensor(
            out=sc2[:], in0=sc2[:], in1=vbuf[:], op=ALU.mult), dep=False)
        at["vbuf_free"] = max(at.get("vbuf_free", 0), v_bv)
        wait("vector", "aS", a_ea)
        wait("vector", "gS", at["g_qb"])
        v_qs = 0
        for b in range(DT):
            v_qs = inc("vector", "vS", lambda v, b=b: v.tensor_tensor_scan(
                out=qbuf[:, b * (C + 1) + 1: b * (C + 1) + 1 + C],
                data0=sc3[:, b * C:(b + 1) * C],
                data1=sc2[:, b * C:(b + 1) * C],
                initial=qbuf[:, b * (C + 1): b * (C + 1) + 1],
                op0=ALU.mult, op1=ALU.add), dep=(b == 0))
        at["v_qs"] = v_qs

    def attn_head_b(j):
        redo = at["redo"]
        pu = at["pu"]
        a_v = at["a_v"]
        t_r = emit_proj(1, at["v_xr"], a_v, mixb3)
        at["t_r_prev"] = t_r[DT - 1]
        wait("scalar", "vS", at.get("last_tail_v", 0))
        a_r = emit_evacs(t_r, rtbuf, AF.Copy)
        at["last_revac_et"] = a_r
        at["a_r"] = a_r
        at["u2_free_g"][pu] = max(at["u2_free_g"][pu], at["v_xr"])
        wait("vector", "aS", a_r[DT - 1])
        v_hr = inc("vector", "vS", lambda v: v.tensor_scalar(
            out=rtbuf[:], in0=rtbuf[:], scalar1=-1.0, scalar2=30.0,
            op0=ALU.mult, op1=ALU.min))
        wait("scalar", "vS", v_hr)
        at["a_er"] = inc("scalar", "aS", lambda s: s.activation(
            out=rtbuf[:], in_=rtbuf[:], func=AF.Exp))

        if redo:
            wait("vector", "gS", max(g_iota, at["v_xr"]))
            run("vector", lambda v: v.tensor_tensor(
                out=r8(sc1), in0=iota[:].unsqueeze(1).broadcast_to([128, DT, C]),
                in1=bc3(tmix[:, W8:W8 + 8]), op=ALU.mult))
            run("vector", lambda v: v.tensor_tensor(
                out=r8(sc1), in0=r8(sc1), in1=bc3(sS(PINW8, PINW8 + 8)),
                op=ALU.add))
            run("vector", lambda v: v.tensor_tensor(
                out=r8(sc2), in0=r8(sc1), in1=v3(pbuf, 1, C), op=ALU.subtract))
            v_pc = inc("vector", "vS", lambda v: v.tensor_tensor(
                out=v3(pbuf, 1, C), in0=v3(pbuf, 1, C), in1=r8(sc1), op=ALU.max))
            wait("scalar", "vS", v_pc)
            run("scalar", lambda s: s.activation(out=sc3[:], in_=sc2[:], func=AF.Relu))
            run("scalar", lambda s: s.activation(
                out=sc3[:], in_=sc3[:], func=AF.Exp, scale=-1.0))
            run("scalar", lambda s: s.activation(
                out=sc2[:], in_=sc2[:], func=AF.Relu, scale=-1.0))
            a_ed = inc("scalar", "aS", lambda s: s.activation(
                out=sc2[:], in_=sc2[:], func=AF.Exp, scale=-1.0))
            wait("vector", "aS", a_ed)
            run("vector", lambda v: v.tensor_tensor(
                out=r8(sc3), in0=r8(sc3), in1=v3(qbuf, 1, C), op=ALU.mult),
                dep=False)
            run("vector", lambda v: v.tensor_tensor(
                out=r8(sc2), in0=r8(sc2), in1=bc3(sS(SQ8, SQ8 + 8)), op=ALU.mult),
                dep=False)
            run("vector", lambda v: v.tensor_tensor(
                out=v3(qbuf, 1, C), in0=r8(sc2), in1=r8(sc3), op=ALU.add))
            run("vector", lambda v: v.tensor_copy(v3(pbuf, 0, 1), sS(SP8, SP8 + 8)))
            run("vector", lambda v: v.tensor_copy(
                v3(qbuf, 0, 1), sS(SQ8, SQ8 + 8)), dep=False)

    def attn_tail(j):
        bu = 0 if at["redo"] and at["j"] == j else j % 3
        g_a = at["g_a"]
        a_r = at["a_r"]
        wait("vector", "gS", g_a)
        v_g = inc("vector", "vS", lambda v: v.tensor_tensor(
            out=r8(sc4), in0=r8(abuf), in1=v3(pbuf, 0, C), op=ALU.subtract))
        at["tail_vg"] = v_g
        run("vector", lambda v: v.tensor_scalar(
            out=sc3[:], in0=sc4[:], scalar1=0.0, scalar2=0.0,
            op0=ALU.max, op1=ALU.bypass))
        v_gn = inc("vector", "vS", lambda v: v.tensor_scalar(
            out=sc4[:], in0=sc4[:], scalar1=0.0, scalar2=0.0,
            op0=ALU.min, op1=ALU.bypass))
        wait("scalar", "vS", v_gn)
        run("scalar", lambda s: s.activation(
            out=sc3[:], in_=sc3[:], func=AF.Exp, scale=-1.0))
        a_esc = inc("scalar", "aS", lambda s: s.activation(
            out=sc4[:], in_=sc4[:], func=AF.Exp))
        wait("vector", "aS", a_esc)
        v_den = inc("vector", "vS", lambda v: v.scalar_tensor_tensor(
            out=abuf[:], in0=sc3[:], scalar=1e-9, in1=sc4[:],
            op0=ALU.add, op1=ALU.add), dep=False)
        wait("vector", "aS", at["a_er"])
        v_d2 = inc("vector", "vS", lambda v: v.scalar_tensor_tensor(
            out=sc2[:], in0=rtbuf[:], scalar=1.0, in1=abuf[:],
            op0=ALU.add, op1=ALU.mult))
        at["abuf_free_v"] = v_d2
        wait("scalar", "vS", v_d2)
        run("scalar", lambda s: s.activation(out=sc2[:], in_=sc2[:], func=AF.Ln))
        a_rd = inc("scalar", "aS", lambda s: s.activation(
            out=sc2[:], in_=sc2[:], func=AF.Exp, scale=-1.0))
        v_n1 = inc("vector", "vS", lambda v: v.tensor_tensor(
            out=r8(sc3), in0=r8(sc3), in1=v3(qbuf, 0, C), op=ALU.mult),
            dep=False)
        at["tail_vn1"] = v_n1
        run("vector", lambda v: v.tensor_tensor(
            out=sc4[:], in0=sc4[:], in1=vbuf[:], op=ALU.mult), dep=False)
        v_num = inc("vector", "vS", lambda v: v.tensor_tensor(
            out=sc3[:], in0=sc3[:], in1=sc4[:], op=ALU.add))
        at["vbuf_free"] = max(at.get("vbuf_free", 0), v_num)
        wait("vector", "aS", a_rd)
        wait("vector", "tS", at.get("wo_t1", 0))
        v_y = inc("vector", "vS", lambda v: v.tensor_tensor(
            out=ybuf[:], in0=sc3[:], in1=sc2[:], op=ALU.mult))
        at["tail_v"][j] = v_y
        at["last_tail_v"] = v_y
        at["wo_stash"][j] = (v_y, bu)

    def attn_wo_mm(j):
        v_y, bu = at["wo_stash"].pop(j)
        wait("tensor", "aS", at["w_ready"].get(3, 0))
        wait("tensor", "vS", v_y)
        wait("tensor", "vS", at["last_resid"])
        wait("tensor", "aS", at["a_r"][3])
        t0 = 0
        for oc in range(2):
            for db in range(DT):
                t0 = inc("tensor", "tS", lambda t, oc=oc, db=db: t.matmul(
                    pE[:, oc * 512:(oc + 1) * 512],
                    lhsT=ybuf[:, db * C: db * C + 128],
                    rhs=slot[3][:, db * 1024 + oc * 512: db * 1024 + oc * 512 + 512],
                    start=(db == 0), stop=(db == DT - 1)), dep=False)
        t1 = 0
        for oc in range(2):
            for db in range(DT):
                t1 = inc("tensor", "tS", lambda t, oc=oc, db=db: t.matmul(
                    pB[:, oc * 512:(oc + 1) * 512],
                    lhsT=ybuf[:, db * C + 128: db * C + 256],
                    rhs=slot[3][:, db * 1024 + oc * 512: db * 1024 + oc * 512 + 512],
                    start=(db == 0), stop=(db == DT - 1)), dep=False)
        at["wores"][j] = (bu, t0, t1)
        at["wo_t1"] = t1

    def attn_wo_res(j):
        bu, t0, t1 = at["wores"].pop(j)
        wait("vector", "tS", t1)
        v_r2a = inc("vector", "vS", lambda v, bu=bu: v.tensor_tensor(
            out=x2c[bu][:, 0:1024], in0=x2c[bu][:, 0:1024],
            in1=pE[:, 0:1024], op=ALU.add))
        v_r2b = inc("vector", "vS", lambda v, bu=bu: v.tensor_tensor(
            out=x2c[bu][:, 1024:2048], in0=x2c[bu][:, 1024:2048],
            in1=pB[:], op=ALU.add), dep=False)
        at["pE_wo_free"] = v_r2a
        at["last_resid"] = v_r2b
        wait("scalar", "vS", v_r2b)
        d_x3 = dma(
            "scalar", DO,
            x3d[C * j: C * (j + 1), :].rearrange("(a p) d -> p a d", p=128),
            x2c[bu][:].rearrange("p (a d) -> p a d", a=2))
        at["x3_store"][j] = d_x3

    dwait("scalar", d_cv)
    wait_consts("vector")
    wait_consts("scalar")
    wait_consts("tensor")
    U0 = st["u"]
    ffn_load(0, mode=1)
    ffn_t1(U0)
    ffn_hmm(U0)
    ffn_load(1, mode=1)
    ffn_stats(U0)
    ffn_t1(U0 + 1)
    ffn_hmm(U0 + 1)
    ffn_load(2, mode=1)
    ffn_stats(U0 + 1)
    ffn_load(3, mode=1)
    emit_wqprep()
    ffn_quant(U0)
    ffn_mm2(U0)
    for k in range(2, 17):
        fu = U0 + k
        ffn_t1(fu)
        ffn_resid(fu - 2)
        ffn_quant(fu - 1)
        ffn_hmm(fu)
        ffn_mm2(fu - 1)
        ffn_stat2(fu - 2)
        if k + 2 < 17:
            ffn_load(k + 2, mode=1)
        ffn_stats(fu)
    ffn_resid(U0 + 15)
    ffn_quant(U0 + 16)
    ffn_mm2(U0 + 16)
    ffn_stat2(U0 + 15)
    ffn_resid(U0 + 16)
    ffn_stat2(U0 + 16)

    wp_t_par = [0, 0]
    wp_last_a = [0]

    def emit_wprep(src, dst_slot, scale, evac_wait_t=0, evac_wait_dO=0):
        stg2 = [abuf[:, 0:1024], abuf[:, 1024:2048]]
        t_by_par = wp_t_par
        wait("sync", "vS", st["vq_all"])
        wait("tensor", "vS", st["last_resid"])
        for et in range(DT):
            par = et % 2
            wait("sync", "tS", t_by_par[par])
            d_l = dma("sync", DX, stg2[par], src[128 * et:128 * et + 128, :])
            dwait("tensor", d_l)
            for half in range(2):
                if wp_last_a[0]:
                    wait("tensor", "aS", wp_last_a[0])
                tq2 = 0
                for b in range(4):
                    ib = half * 4 + b
                    tq2 = inc("tensor", "tS", lambda t, ib=ib, b=b, par=par:
                              t.transpose(pB[:, b * 128:(b + 1) * 128],
                                          in_=stg2[par][:, ib * 128:(ib + 1) * 128],
                                          identity=eye32[:]), dep=False)
                t_by_par[par] = tq2
                wait("scalar", "tS", tq2)
                if evac_wait_t:
                    wait("scalar", "tS", evac_wait_t)
                if evac_wait_dO:
                    dwait("scalar", evac_wait_dO)
                wp_last_a[0] = inc("scalar", "aS", lambda s, et=et, half=half,
                                   d=dst_slot, sc=scale: s.activation(
                    out=slot[d][:].rearrange("p (b c) -> p b c", b=DT)[
                        :, half * 4:half * 4 + 4, et * 128:(et + 1) * 128],
                    in_=pB[:, 0:512].rearrange("p (b c) -> p b c", b=4),
                    func=AF.Copy, scale=sc))
        return wp_last_a[0]

    ffn1_last_t = cnt["tS"]
    at["sc1_dve_free"] = cnt["vS"]
    at["mix_pe_free"] = cnt["tS"]
    attn_u2(0)
    at["w_ready"][0] = emit_wprep(wk_in, 0, 1.0, evac_wait_t=ffn1_last_t)
    attn_kxx_k(0)
    at["w_ready"][2] = emit_wprep(wv_in, 2, 1.0, evac_wait_dO=st["d_wq_stash"])
    attn_kxx_v(0)
    at["w_ready"][1] = emit_wprep(wr_in, 1, 1.0)
    attn_mixr(0)
    attn_u2(1)
    at["w_ready"][3] = emit_wprep(wo_in, 3, 1.0)
    at["wprep_t"] = cnt["tS"]
    attn_head_late(0)
    attn_head_b(0)
    for j in range(1, NCH):
        attn_kxx_k(j)
        attn_kxx_v(j)
        attn_mixr(j)
        if j + 1 < NCH:
            attn_u2(j + 1)
        attn_head_late(j)
        attn_head_b(j)
        if j >= 2:
            attn_wo_mm(j - 1)
            attn_wo_res(j - 1)
        attn_tail(j)
    attn_wo_mm(NCH - 1)
    attn_wo_res(NCH - 1)

    run("vector", lambda v: v.tensor_copy(sumb[:, 0:8], v3(pbuf, C, 1)))
    v_sum = inc("vector", "vS", lambda v: v.tensor_copy(
        sumb[:, 8:16], v3(qbuf, C, 1)), dep=False)
    wait("scalar", "vS", v_sum)
    d_sum = dma("scalar", DO, cc_in.ap(), sumb[:])
    dwait("gpsimd", d_sum)
    cnt["dC"] += 1
    ops["gpsimd"].append(("o", lambda g: g.collective_compute(
        "AllGather", ALU.bypass,
        replica_groups=[[0, 1, 2, 3], [4, 5, 6, 7]],
        ins=[cc_in.ap()], outs=[cc_out.ap()]).then_inc(sems["dC"])))
    wait("sync", "dC", cnt["dC"])
    d_gs = []
    for r in range(4):
        d_gs.append(dma("sync", DX, gath[:, 16 * r:16 * r + 16],
                        cc_out.ap()[128 * r:128 * r + 128, :]))
    for t in d_gs:
        dwait("gpsimd", t)
    run("gpsimd", lambda g: g.memset(sS(SP8, SP8 + 8), NEG))
    g_f = inc("gpsimd", "gS", lambda g: g.memset(sS(SQ8, SQ8 + 8), 0.0))
    for t in d_gs:
        dwait("vector", t)
    wait("vector", "gS", g_f)
    for jj in range(3):
        run("vector", lambda v, jj=jj: v.tensor_scalar(
            out=sS(FA8, FA8 + 8), in0=gath[:, 16 * jj:16 * jj + 8],
            scalar1=cmask[:, jj:jj + 1], scalar2=cmask[:, 3 + jj:4 + jj],
            op0=ALU.mult, op1=ALU.add))
        run("vector", lambda v, jj=jj: v.tensor_scalar(
            out=sS(FB8, FB8 + 8), in0=gath[:, 16 * jj + 8:16 * jj + 16],
            scalar1=cmask[:, jj:jj + 1], scalar2=0.0,
            op0=ALU.mult, op1=ALU.bypass), dep=False)
        run("vector", lambda v, jj=jj: v.tensor_scalar(
            out=sS(FC8, FC8 + 8), in0=tmix[:, W8:W8 + 8],
            scalar1=cmask[:, 6 + jj:7 + jj], scalar2=0.0,
            op0=ALU.mult, op1=ALU.bypass), dep=False)
        run("vector", lambda v: v.tensor_tensor(
            out=sS(FC8, FC8 + 8), in0=sS(FC8, FC8 + 8), in1=sS(SP8, SP8 + 8),
            op=ALU.add))
        run("vector", lambda v: v.tensor_tensor(
            out=sS(FD8, FD8 + 8), in0=sS(FA8, FA8 + 8), in1=sS(FC8, FC8 + 8),
            op=ALU.max))
        run("vector", lambda v: v.tensor_tensor(
            out=sS(FA8, FA8 + 8), in0=sS(FA8, FA8 + 8), in1=sS(FD8, FD8 + 8),
            op=ALU.subtract))
        v_f = inc("vector", "vS", lambda v: v.tensor_tensor(
            out=sS(FC8, FC8 + 8), in0=sS(FC8, FC8 + 8), in1=sS(FD8, FD8 + 8),
            op=ALU.subtract))
        wait("scalar", "vS", v_f)
        run("scalar", lambda s: s.activation(
            out=sS(FA8, FA8 + 8), in_=sS(FA8, FA8 + 8), func=AF.Exp))
        a_f = inc("scalar", "aS", lambda s: s.activation(
            out=sS(FC8, FC8 + 8), in_=sS(FC8, FC8 + 8), func=AF.Exp))
        wait("vector", "aS", a_f)
        run("vector", lambda v: v.tensor_tensor(
            out=sS(FB8, FB8 + 8), in0=sS(FB8, FB8 + 8), in1=sS(FA8, FA8 + 8),
            op=ALU.mult))
        run("vector", lambda v: v.tensor_tensor(
            out=sS(FC8, FC8 + 8), in0=sS(SQ8, SQ8 + 8), in1=sS(FC8, FC8 + 8),
            op=ALU.mult))
        run("vector", lambda v: v.tensor_tensor(
            out=sS(SQ8, SQ8 + 8), in0=sS(FB8, FB8 + 8), in1=sS(FC8, FC8 + 8),
            op=ALU.add))
        run("vector", lambda v: v.tensor_copy(sS(SP8, SP8 + 8), sS(FD8, FD8 + 8)))
    inc("vector", "vS", lambda v: v.tensor_tensor(
        out=sS(PINW8, PINW8 + 8), in0=sS(SP8, SP8 + 8), in1=tmix[:, W8:W8 + 8],
        op=ALU.add))

    attn_u2(0, redo=True)
    attn_kxx_k(0, redo=True)
    attn_kxx_v(0)
    attn_mixr(0)
    attn_head_late(0)
    attn_head_b(0)

    wait("sync", "tS", at["t_k_prev"])
    d_h2 = dma("sync", DX, slot[0][:].rearrange("p (b j) -> p b j", b=DT),
               h_in.rearrange("(b p) j -> p b j", p=128))
    wait("sync", "tS", at["t_r_prev"])
    dwait("sync", st["d_wq_stash"])
    d_wq2 = dma("sync", DX, slot[1][:], wqd)
    dwait("tensor", d_h2)
    dwait("tensor", d_wq2)
    st["last_t1evac"] = at["u2_prev_evac"]
    st["last_t2evac"] = at["u2_prev_evac"]
    st["half_a"] = [at["a_r"][3], at["a_r"][7]]
    wait("scalar", "tS", at["t_v_prev"])
    wait("scalar", "gS", max(at["v_xr"], at["v_xv"]))
    xts2 = [x2c[1][:, 0:1024], x2c[1][:, 1024:2048],
            x2c[2][:, 0:1024], x2c[2][:, 1024:2048]]
    units2 = list(range(3, 17)) + [1, 2]
    U2 = st["u"]

    def ffn_load2(k):
        unit = units2[k]
        kw = dict(xt_override=xts2[k % 4])
        if k == 0:
            kw.update(prev_u=U2 - 4, extra_dw=[at["d_xh"]],
                      extra_aS=at["x2c_a"][1])
        elif k == 1:
            kw.update(prev_u=U2 - 2)
        elif k in (2, 3):
            kw.update(prev_u=-1, extra_dw=[at["x3_store"][5]],
                      extra_aS=at["x2c_a"][2])
        if unit in (1, 2):
            kw.setdefault("extra_dw", []).append(at["x3_store"][0])
        ffn_load(unit, 2, **kw)

    ffn_load2(0)
    ffn_t1(U2)
    ffn_hmm(U2)
    ffn_load2(1)
    ffn_stats(U2)
    ffn_load2(2)
    ffn_load2(3)
    attn_tail(0)
    attn_wo_mm(0)
    attn_wo_res(0)
    st["last_resid"] = at["last_resid"]
    st["half_v"] = [at["pE_wo_free"], 0]
    ffn_quant(U2)
    ffn_mm2(U2)
    ffn_t1(U2 + 1)
    ffn_hmm(U2 + 1)
    ffn_stats(U2 + 1)
    for k in range(2, 16):
        fu = U2 + k
        ffn_t1(fu)
        ffn_resid(fu - 2)
        ffn_quant(fu - 1)
        ffn_hmm(fu)
        ffn_mm2(fu - 1)
        ffn_stat2(fu - 2)
        if k + 2 < 16:
            ffn_load2(k + 2)
        ffn_stats(fu)
    ffn_resid(U2 + 14)
    ffn_quant(U2 + 15)
    ffn_mm2(U2 + 15)
    ffn_stat2(U2 + 14)
    ffn_resid(U2 + 15)
    ffn_stat2(U2 + 15)

    dwait_all("gpsimd")
    dwait_all("sync")

    def _mk(eng_ops, auto_drain):
        def f(e):
            pending = False
            for kind, fn in eng_ops:
                if kind == "o" and auto_drain and pending:
                    e.drain()
                    pending = False
                fn(e)
                if kind in ("o", "i"):
                    pending = True
        return f

    with nc.Block() as blk:
        blk.sync(_mk(ops["sync"], False))
        blk.tensor(_mk(ops["tensor"], False))
        blk.vector(_mk(ops["vector"], True))
        blk.scalar(_mk(ops["scalar"], True))
        blk.gpsimd(_mk(ops["gpsimd"], True))
    return nc


def _hadamard_bf16():
    h = np.array([[1.0]], dtype=np.float64)
    while h.shape[0] < D:
        h = np.block([[h, h], [h, -h]])
    return (h / np.sqrt(D)).astype(np.float32).astype(ml_dtypes.bfloat16)


def kernel(L0_attn_x, L0_attn_time_mix_k, L0_attn_time_mix_v, L0_attn_time_mix_r,
           L0_attn_W_k, L0_attn_W_v, L0_attn_W_r, L0_attn_W_o,
           L0_attn_time_decay, L0_attn_time_first, L0_ffn_weight):
    x = np.ascontiguousarray(np.asarray(L0_attn_x, np.float32))
    Wk = np.ascontiguousarray(np.asarray(L0_attn_W_k, np.float32))
    Wv = np.ascontiguousarray(np.asarray(L0_attn_W_v, np.float32))
    Wr = np.ascontiguousarray(np.asarray(L0_attn_W_r, np.float32))
    Wo = np.ascontiguousarray(np.asarray(L0_attn_W_o, np.float32))
    Wf = np.ascontiguousarray(np.asarray(L0_ffn_weight, np.float32))
    tmk = np.asarray(L0_attn_time_mix_k, np.float32)
    tmv = np.asarray(L0_attn_time_mix_v, np.float32)
    tmr = np.asarray(L0_attn_time_mix_r, np.float32)
    w = np.asarray(L0_attn_time_decay, np.float32)
    u = np.asarray(L0_attn_time_first, np.float32)

    def fm8(vec):
        return np.ascontiguousarray(vec.reshape(DT, 128).T)

    tmix = np.zeros((128, 40), np.float32)
    tmix[:, 0:8] = fm8(tmk)
    tmix[:, 8:16] = fm8(tmv)
    tmix[:, 16:24] = fm8(tmr)
    tmix[:, 24:32] = fm8(w)
    tmix[:, 32:40] = fm8(u)
    Hb = _hadamard_bf16()
    eye32 = np.eye(128, dtype=np.float32)
    eyebf = np.eye(128, dtype=np.float32).astype(ml_dtypes.bfloat16)
    constv = np.zeros((128, 3), np.float32)
    constv[:, 0] = BIGEPS
    constv[:, 1] = LN_D
    constv[:, 2] = EPS_LN

    nc = build_nc()
    in_maps = []
    for c in range(8):
        b, qd = c // 4, c % 4
        lo = qd * TSH
        shard = np.zeros((TSH + 1, D), np.float32)
        shard[1:] = x[b, lo:lo + TSH]
        if qd > 0:
            shard[0] = x[b, lo - 1]
        cm = np.zeros((128, 9), np.float32)
        for jj in range(3):
            m = 1.0 if jj < qd else 0.0
            cm[:, jj] = m
            cm[:, 3 + jj] = (m - 1.0) * 1e30
            cm[:, 6 + jj] = m * TSH
        in_maps.append({
            "x": shard, "wk": Wk, "wv": Wv, "wr": Wr, "wo": Wo, "wf": Wf,
            "hmat": Hb, "tmix": tmix, "cmask": cm, "eye32": eye32, "eyebf": eyebf,
            "constv": constv,
        })
    res = run_bass_kernel_spmd(nc, in_maps, core_ids=list(range(8)))
    out = np.empty((2, 8192, D), np.float32)
    for c, r in enumerate(res.results):
        out[c // 4, (c % 4) * TSH:(c % 4 + 1) * TSH] = r["out"]
    return out
